# version 07
# speedup vs baseline: 1.7428x; 1.0013x over previous
import numpy as np
import ml_dtypes
from contextlib import ExitStack

import concourse.bass as bass
import concourse.mybir as mybir
from concourse.bass_utils import run_bass_kernel_spmd
from concourse.bass_core_rust import IndirectOffsetOnAxis

F32 = mybir.dt.float32
BF16 = mybir.dt.bfloat16
FP8 = mybir.dt.float8e4
I32 = mybir.dt.int32
AF = mybir.ActivationFunctionType
ALU = mybir.AluOpType
AX = mybir.AxisListType.X

NCORES = 8
T = 8192
D = 2048
NT = 64
LT = 8
CAP = 2560
NH = 16
HD = 128
I_FFN = 6144
J_SH = 5632
I_EXP = 1408
EPS = 1e-5
BF = ml_dtypes.bfloat16


def g_of(c, i):
    return c + 8 * i


class Sem:
    __slots__ = ("h", "n")

    def __init__(self, h):
        self.h = h
        self.n = 0


class Prog:
    ENGS = ("sync", "tensor", "vector", "scalar", "gpsimd")

    def __init__(self, nc, es):
        self.nc = nc
        self.es = es
        self.q = {e: [] for e in self.ENGS}

    def sem(self, name):
        return Sem(self.es.enter_context(self.nc.semaphore(name)))

    def op(self, eng, make, waits=(), inc=None, inc_n=1):
        w = tuple((s.h, v) for (s, v) in waits if v > 0)
        ih = None
        if inc is not None:
            inc.n += inc_n
            ih = inc.h

        def f(e, make=make, w=w, ih=ih, iv=inc_n):
            for (h, v) in w:
                e.wait_ge(h, v)
            ins = make(e)
            if ih is not None:
                ins.then_inc(ih, iv)
        self.q[eng].append(f)

    def wait(self, eng, *waits):
        w = tuple((s.h, v) for (s, v) in waits if v > 0)

        def f(e, w=w):
            for (h, v) in w:
                e.wait_ge(h, v)
        self.q[eng].append(f)

    def finish(self, blk):
        for eng in self.ENGS:
            fns = self.q[eng]
            if not fns:
                continue
            getattr(blk, eng)(lambda e, fns=fns: [f(e) for f in fns])


def run_spmd(nc, in_maps, **kw):
    return run_bass_kernel_spmd(nc, in_maps, core_ids=list(range(NCORES)), **kw)



def emit_layernorm(p, stream_tile, lnscr, sqdump, col8, s_v, s_s, out_bf=None,
                   bf_war=None):
    c_sum = col8[:, 0:1]
    c_var = col8[:, 1:2]
    c_tmp = col8[:, 2:3]
    c_rstd = col8[:, 3:4]
    p.op("vector", lambda e: e.reduce_sum(out=c_sum, in_=stream_tile, axis=AX),
         waits=[(s_s, s_s.n)], inc=s_v)
    p.op("vector", lambda e: e.tensor_scalar_mul(out=c_sum, in0=c_sum,
                                                 scalar1=-1.0 / D),
         waits=[(s_v, s_v.n)], inc=s_v)
    p.op("scalar", lambda e: e.activation(out=lnscr, in_=stream_tile,
                                          func=AF.Identity, bias=c_sum),
         waits=[(s_v, s_v.n)], inc=s_s)
    p.op("scalar", lambda e: e.activation(out=sqdump, in_=lnscr,
                                          func=AF.Square, accum_out=c_var),
         waits=[(s_s, s_s.n)], inc=s_s)
    p.op("vector", lambda e: e.tensor_scalar(out=c_tmp, in0=c_var,
                                             scalar1=1.0 / D, scalar2=EPS,
                                             op0=ALU.mult, op1=ALU.add),
         waits=[(s_s, s_s.n)], inc=s_v)
    p.op("vector", lambda e: e.reciprocal(out=c_tmp, in_=c_tmp),
         waits=[(s_v, s_v.n)], inc=s_v)
    p.op("scalar", lambda e: e.activation(out=c_rstd, in_=c_tmp, func=AF.Sqrt),
         waits=[(s_v, s_v.n)], inc=s_s)
    p.op("vector", lambda e: e.tensor_scalar_mul(out=stream_tile, in0=lnscr,
                                                 scalar1=c_rstd),
         waits=[(s_s, s_s.n)], inc=s_v)
    if out_bf is not None:
        w = [(s_v, s_v.n), (s_s, s_s.n)]
        if bf_war is not None:
            w.append((bf_war[0], bf_war[0].n))
        p.op("scalar", lambda e: e.activation(out=out_bf, in_=lnscr,
                                              func=AF.Identity, scale=c_rstd),
             waits=w, inc=s_s)


def emit_tile_transposes(p, nc, src_bf, xT, i, id_bf, p_t, s_mm, s_v, s_s,
                         src_ready_s):
    grp_v = {}
    for g4 in range(4):
        jj = g4 % 2
        for t in range(4):
            cc = 4 * g4 + t
            w = []
            if t == 0:
                if g4 <= 1:
                    w = [(s_s, src_ready_s), (s_v, s_v.n)]
                else:
                    w = [(s_v, grp_v[g4 - 2])]
            p.op("tensor", lambda e, jj=jj, t=t, cc=cc: e.transpose(
                p_t[jj][:, t * 128:(t + 1) * 128],
                src_bf[:, cc * 128:(cc + 1) * 128], id_bf[:]),
                waits=w, inc=s_mm if t == 3 else None)
        mmv = s_mm.n
        for t in range(4):
            cc = 4 * g4 + t
            p.op("vector", lambda e, jj=jj, t=t, cc=cc: e.tensor_copy(
                out=xT[cc][:, i * 128:(i + 1) * 128],
                in_=p_t[jj][:, t * 128:(t + 1) * 128]),
                waits=[(s_mm, mmv)] if t == 0 else [],
                inc=s_v if t == 3 else None)
        grp_v[g4] = s_v.n



def prep_host(inputs):
    x = np.ascontiguousarray(np.asarray(inputs["L0_attn_x"])[0])
    bv = np.asarray(inputs["L0_attn_block_values"])
    cols = np.asarray(inputs["L0_attn_cols"]).astype(np.int64)
    rows = np.asarray(inputs["L0_attn_rows"]).astype(np.int64)

    per_core = [dict() for _ in range(NCORES)]
    for c in range(NCORES):
        tiles = [x[g_of(c, i) * 128:(g_of(c, i) + 1) * 128, :] for i in range(LT)]
        per_core[c]["x_own"] = np.ascontiguousarray(np.concatenate(tiles, 0))

    slots = [[] for _ in range(NCORES)]
    for b in range(bv.shape[0]):
        r = int(rows[b])
        gt = r // 2
        c = gt % 8
        i = gt // 8
        lr = 2 * i + (r % 2)
        slots[c].append((lr, b))
    B_max = max(1, max(len(s) for s in slots))
    for c in range(NCORES):
        sl = sorted(slots[c], key=lambda t: t[0])
        cx = np.zeros((B_max, 64, D), np.float32)
        cw = np.zeros((B_max, 16, 64, 64), np.float32)
        for s, (lr, b) in enumerate(sl):
            src = int(cols[b])
            cx[s] = x[src * 64:(src + 1) * 64, :]
            cw[s] = bv[b].transpose(2, 1, 0)
        per_core[c]["conv_x"] = cx
        per_core[c]["conv_w"] = cw
        begin = np.zeros(16, np.int64)
        end = np.zeros(16, np.int64)
        pos = 0
        lrs = [lr for (lr, b) in sl]
        for lr in range(16):
            begin[lr] = pos
            while pos < len(lrs) and lrs[pos] == lr:
                pos += 1
            end[lr] = pos
        gidx = np.zeros((128, 2 * LT), np.int32)
        for i in range(LT):
            for pp in range(128):
                lr = 2 * i + pp // 64
                w = pp % 64
                gidx[pp, i] = end[lr] * 64 + w
                gidx[pp, LT + i] = begin[lr] * 64 + w
        per_core[c]["conv_gidx"] = gidx

    shared_names = [
        ("gate_w", "L0_ffn_gate_w"), ("up_w", "L0_ffn_up_w"),
        ("down_w", "L0_ffn_down_w"),
        ("gate_s", "L0_ffn_gate_scale"), ("up_s", "L0_ffn_up_scale"),
        ("down_s", "L0_ffn_down_scale"),
        ("wqkv", "L1_attn_Wqkv"), ("bqkv", "L1_attn_bqkv"),
        ("wo", "L1_attn_Wo"), ("bo", "L1_attn_bo"),
        ("router", "L1_ffn_router"),
        ("w_sug", "L1_ffn_shared_w_up_gate"), ("w_sd", "L1_ffn_shared_w_down"),
    ]
    shared = {k: np.ascontiguousarray(np.asarray(inputs[n])) for k, n in shared_names}
    shared["scol_in"] = np.ascontiguousarray(np.concatenate(
        [shared["gate_s"].reshape(48, 128).T,
         shared["up_s"].reshape(48, 128).T], axis=1).astype(np.float32))
    wug = np.asarray(inputs["L1_ffn_w_up_gate"])
    wdn = np.asarray(inputs["L1_ffn_w_down"])
    for c in range(NCORES):
        per_core[c]["w_ug_e"] = np.ascontiguousarray(wug[c])
        per_core[c]["w_dn_e"] = np.ascontiguousarray(wdn[c])
        me = np.zeros((128, 8), np.float32)
        me[:, c] = 1.0
        per_core[c]["my_e"] = me
    return per_core, shared, B_max


def assemble_output(parts):
    out = np.zeros((T, D), np.float32)
    for c in range(NCORES):
        for i in range(LT):
            out[g_of(c, i) * 128:(g_of(c, i) + 1) * 128, :] = \
                parts[c][i * 128:(i + 1) * 128, :]
    return out[None]



def build_launch1(B_max, debug=False):
    nc = bass.Bass("TRN2", target_bir_lowering=False)
    x_own = nc.dram_tensor("x_own", [LT * 128, D], F32, kind="ExternalInput").ap()
    conv_x = nc.dram_tensor("conv_x", [B_max, 64, D], F32, kind="ExternalInput").ap()
    conv_w = nc.dram_tensor("conv_w", [B_max, 16, 64, 64], F32, kind="ExternalInput").ap()
    conv_gidx = nc.dram_tensor("conv_gidx", [128, 2 * LT], I32, kind="ExternalInput").ap()
    gate_w = nc.dram_tensor("gate_w", [D, I_FFN], F32, kind="ExternalInput").ap()
    up_w = nc.dram_tensor("up_w", [D, I_FFN], F32, kind="ExternalInput").ap()
    down_w = nc.dram_tensor("down_w", [I_FFN, D], F32, kind="ExternalInput").ap()
    gate_s = nc.dram_tensor("gate_s", [I_FFN], F32, kind="ExternalInput").ap()
    up_s = nc.dram_tensor("up_s", [I_FFN], F32, kind="ExternalInput").ap()
    down_s = nc.dram_tensor("down_s", [D], F32, kind="ExternalInput").ap()
    wqkv = nc.dram_tensor("wqkv", [D, 3 * D], F32, kind="ExternalInput").ap()
    bqkv = nc.dram_tensor("bqkv", [3 * D], F32, kind="ExternalInput").ap()

    stream_out = nc.dram_tensor("stream_out", [LT * 128, D], F32, kind="ExternalOutput").ap()
    q_out = nc.dram_tensor("q_out", [D, LT * 128], BF16, kind="ExternalOutput").ap()
    k_out = nc.dram_tensor("k_out", [D, LT * 128], BF16, kind="ExternalOutput").ap()
    v_out = nc.dram_tensor("v_out", [LT * 128, D], BF16, kind="ExternalOutput").ap()

    if debug:
        pre_dbg = nc.dram_tensor("pre_dbg", [LT * 128, D], F32, kind="ExternalOutput").ap()
        h_dbg = nc.dram_tensor("h_dbg", [I_FFN, LT * 128], BF16, kind="ExternalOutput").ap()
        mix_dbg = nc.dram_tensor("mix_dbg", [LT * 128, D], F32, kind="ExternalOutput").ap()
        x1_dbg = nc.dram_tensor("x1_dbg", [LT * 128, D], F32, kind="ExternalOutput").ap()
        cpre_dbg = nc.dram_tensor("cpre_dbg", [(B_max + 1) * 64, D], F32, kind="ExternalOutput").ap()
    c_dram = nc.dram_tensor("c_dram", [(B_max + 1) * 64, D], F32).ap()
    h_dram = nc.dram_tensor("h_dram", [I_FFN, LT * 128], BF16).ap()

    id_bf_d = nc.inline_tensor(np.eye(128, dtype=BF), "id_bf").ap()
    ones_bf_d = nc.inline_tensor(np.ones((1, 512), dtype=BF), "ones_bf").ap()
    ones_f_d = nc.inline_tensor(np.ones((1, 512), dtype=np.float32), "ones_f").ap()

    es = ExitStack()
    sbt = lambda name, shape, dt: es.enter_context(nc.sbuf_tensor(name, shape, dt))

    stream = [sbt(f"stream{i}", [128, D], F32) for i in range(LT)]
    xT = [sbt(f"xT{c}", [128, LT * 128], BF16) for c in range(16)]
    xbf_scr = sbt("xbf_scr", [128, D], BF16)
    lnscr = sbt("lnscr", [128, D], F32)
    sqdump = sbt("sqdump", [128, D], BF16)
    col8 = sbt("col8", [128, 8], F32)
    id_bf = sbt("id_bf_s", [128, 128], BF16)
    ones_bf = sbt("ones_bf_s", [1, 512], BF16)
    ones_f = sbt("ones_f_s", [1, 512], F32)
    gidx_s = sbt("gidx_s", [128, 2 * LT], I32)

    p = Prog(nc, es)
    s_ld = p.sem("s_ld")
    s_gld = p.sem("s_gld")
    s_mm = p.sem("s_mm")
    s_v = p.sem("s_v")
    s_s = p.sem("s_s")
    s_st = p.sem("s_st")
    s_pf = p.sem("s_pf")

    blk = es.enter_context(nc.Block())
    with es:
        for i in range(LT):
            p.op("sync", lambda e, i=i: e.dma_start(
                out=stream[i][:], in_=x_own[i * 128:(i + 1) * 128, :]),
                inc=s_ld, inc_n=16)
        p.op("sync", lambda e: e.dma_start(out=id_bf[:], in_=id_bf_d), inc=s_ld, inc_n=16)
        p.op("sync", lambda e: e.dma_start(out=ones_bf[:], in_=ones_bf_d), inc=s_ld, inc_n=16)
        p.op("sync", lambda e: e.dma_start(out=ones_f[:], in_=ones_f_d), inc=s_ld, inc_n=16)
        p.op("sync", lambda e: e.dma_start(out=gidx_s[:], in_=conv_gidx),
             inc=s_ld, inc_n=16)
        ld_base = s_ld.n

        with ExitStack() as ces:
            cs = lambda name, shape, dt: ces.enter_context(nc.sbuf_tensor(name, shape, dt))
            cxs = [cs(f"cxs{j}", [64, 16 + D], BF16) for j in range(2)]
            cws = [cs(f"cws{j}", [64, 16 * 64], BF16) for j in range(2)]
            cpref = [cs(f"cpref{j}", [64, D], F32) for j in range(2)]
            gbufA = cs("gbufA", [128, D], F32)
            gbufB = cs("gbufB", [128, D], F32)
            p_conv = ces.enter_context(nc.psum_tensor("p_conv", [64, D], F32))
            p_t = [ces.enter_context(nc.psum_tensor(f"p_t{j}", [128, 512], BF16))
                   for j in range(2)]

            p.op("vector", lambda e: e.memset(cpref[1][:], 0.0), inc=s_v)
            zv = s_v.n
            p.op("vector", lambda e: e.memset(cxs[0][:, 0:16], 0.0), inc=s_v)
            p.op("vector", lambda e: e.memset(cxs[1][:, 0:16], 0.0), inc=s_v)
            zx = s_v.n
            p.op("sync", lambda e: e.dma_start(out=c_dram[0:64, :], in_=cpref[1][:]),
                 waits=[(s_v, zv)], inc=s_pf, inc_n=16)

            conv_mm_done = {}
            conv_pref_done = {}
            gld0 = s_gld.n

            def load_slot(s):
                j = s % 2
                wv = conv_mm_done.get(s - 2, 0)
                p.op("gpsimd", lambda e, s=s, j=j: e.dma_start(
                    out=cxs[j][:, 16:], in_=conv_x[s]),
                    waits=[(s_mm, wv), (s_v, zx)], inc=s_gld, inc_n=16)
                p.op("gpsimd", lambda e, s=s, j=j: e.dma_start(
                    out=cws[j][:].rearrange("j (k i) -> j k i", k=16),
                    in_=conv_w[s].rearrange("k j i -> j k i")),
                    inc=s_gld, inc_n=16)

            load_slot(0)
            if B_max > 1:
                load_slot(1)
            for s in range(B_max):
                j = s % 2
                ldv = gld0 + 32 * min(s + 2, B_max)
                for ch in range(4):
                    for k in range(16):
                        lhsT = cws[j][:, k * 64:(k + 1) * 64]
                        rhs = cxs[j][:, 1 + k + ch * 512: 1 + k + ch * 512 + 512]
                        st = (k == 0)
                        sp = (k == 15)
                        waits = []
                        if ch == 0 and k == 0:
                            waits = [(s_gld, ldv),
                                     (s_v, conv_pref_done.get(s - 1, 0))]
                        last = (sp and ch == 3)
                        p.op("tensor", lambda e, lhsT=lhsT, rhs=rhs, st=st, sp=sp, ch=ch:
                             e.matmul(p_conv[:, ch * 512:(ch + 1) * 512], lhsT,
                                      rhs, start=st, stop=sp),
                             waits=waits, inc=s_mm if last else None)
                conv_mm_done[s] = s_mm.n
                prev = cpref[(s + 1) % 2]
                cur = cpref[s % 2]
                for ch in range(4):
                    waits = []
                    if ch == 0:
                        waits = [(s_mm, conv_mm_done[s]), (s_pf, 16 * s)]
                    if s == 0:
                        p.op("vector", lambda e, cur=cur, ch=ch: e.tensor_copy(
                            out=cur[:, ch * 512:(ch + 1) * 512],
                            in_=p_conv[:, ch * 512:(ch + 1) * 512]),
                            waits=waits, inc=s_v)
                    else:
                        p.op("vector", lambda e, cur=cur, prev=prev, ch=ch:
                             e.tensor_add(
                                 out=cur[:, ch * 512:(ch + 1) * 512],
                                 in0=p_conv[:, ch * 512:(ch + 1) * 512],
                                 in1=prev[:, ch * 512:(ch + 1) * 512]),
                             waits=waits, inc=s_v)
                conv_pref_done[s] = s_v.n
                p.op("sync", lambda e, s=s, cur=cur: e.dma_start(
                    out=c_dram[(s + 1) * 64:(s + 2) * 64, :], in_=cur[:]),
                    waits=[(s_v, conv_pref_done[s])], inc=s_pf, inc_n=16)
                if s + 2 < B_max:
                    load_slot(s + 2)

            pf_all = s_pf.n
            if debug:
                p.op("sync", lambda e: e.dma_start(out=cpre_dbg, in_=c_dram),
                     waits=[(s_pf, pf_all)], inc=s_st, inc_n=16)
            for i in range(LT):
                p.op("gpsimd", lambda e, i=i: e.indirect_dma_start(
                    out=gbufA[:], out_offset=None, in_=c_dram,
                    in_offset=IndirectOffsetOnAxis(gidx_s[:, i:i + 1], 0),
                    bounds_check=(B_max + 1) * 64 - 1, oob_is_err=True),
                    waits=[(s_pf, pf_all), (s_ld, ld_base), (s_v, s_v.n)],
                    inc=s_gld, inc_n=16)
                p.op("gpsimd", lambda e, i=i: e.indirect_dma_start(
                    out=gbufB[:], out_offset=None, in_=c_dram,
                    in_offset=IndirectOffsetOnAxis(gidx_s[:, LT + i:LT + i + 1], 0),
                    bounds_check=(B_max + 1) * 64 - 1, oob_is_err=True),
                    inc=s_gld, inc_n=16)
                gv = s_gld.n
                p.op("vector", lambda e: e.tensor_sub(out=gbufA[:], in0=gbufA[:],
                                                      in1=gbufB[:]),
                     waits=[(s_gld, gv)], inc=s_v)
                p.wait("vector", (s_v, s_v.n))
                if debug:
                    p.op("sync", lambda e, i=i: e.dma_start(
                        out=mix_dbg[i * 128:(i + 1) * 128, :], in_=gbufA[:]),
                        waits=[(s_v, s_v.n)], inc=s_st, inc_n=16)
                p.op("vector", lambda e, i=i: e.tensor_add(
                    out=stream[i][:], in0=stream[i][:], in1=gbufA[:]),
                    waits=[(s_st, s_st.n)] if debug else [], inc=s_v)
                emit_layernorm(p, stream[i][:], lnscr[:], sqdump[:], col8[:],
                               s_v, s_s, out_bf=xbf_scr[:], bf_war=(s_mm,))
                if debug:
                    p.op("sync", lambda e, i=i: e.dma_start(
                        out=x1_dbg[i * 128:(i + 1) * 128, :], in_=stream[i][:]),
                        waits=[(s_v, s_v.n)], inc=s_st, inc_n=16)
                    p.wait("vector", (s_st, s_st.n))
                emit_tile_transposes(p, nc, xbf_scr[:], xT, i, id_bf[:], p_t,
                                     s_mm, s_v, s_s, s_s.n)
            conv_end_mm = s_mm.n
            conv_end_g = s_gld.n

        with ExitStack() as fes:
            fs = lambda name, shape, dt: fes.enter_context(nc.sbuf_tensor(name, shape, dt))
            fp = lambda name, shape, dt: fes.enter_context(nc.psum_tensor(name, shape, dt))
            scol = fs("scol", [128, 96], F32)
            dsrow = [fs(f"dsrow{j}", [128, 512], F32) for j in range(4)]

            with ExitStack() as aes:
                asb = lambda name, shape, dt: aes.enter_context(nc.sbuf_tensor(name, shape, dt))
                aps = lambda name, shape, dt: aes.enter_context(nc.psum_tensor(name, shape, dt))
                wg = [asb(f"wg{j}", [128, D], BF16) for j in range(2)]
                wu = [asb(f"wu{j}", [128, D], BF16) for j in range(2)]
                wg8 = [asb(f"wg8{j}", [128, D], FP8) for j in range(2)]
                wu8 = [asb(f"wu8{j}", [128, D], FP8) for j in range(2)]
                fuse = asb("fuse", [128, LT * 128], BF16)
                srow = asb("srow", [1, D], F32)
                p_g = [aps(f"p_g{j}", [128, 512], F32) for j in range(2)]
                p_u = [aps(f"p_u{j}", [128, 512], F32) for j in range(2)]
                p_sc = aps("p_sc", [128, 512], F32)

                ffn_bar = [(s_mm, s_mm.n), (s_v, s_v.n), (s_s, s_s.n)]
                p.op("sync", lambda e: e.dma_start(
                    out=scol[:, 0:96], in_=scol_in),
                    waits=list(ffn_bar), inc=s_ld, inc_n=16)
                scol_ld = s_ld.n
                p.op("sync", lambda e: e.dma_start(out=srow[:], in_=down_s[None, :]),
                     waits=[(s_mm, s_mm.n)], inc=s_ld, inc_n=16)
                dsl = s_ld.n
                for jj in range(4):
                    p.op("tensor", lambda e, jj=jj: e.matmul(
                        p_sc[:], ones_f[0:1, 0:128],
                        srow[0:1, jj * 512:(jj + 1) * 512], start=True, stop=True),
                        waits=[(s_ld, dsl), (s_v, s_v.n)], inc=s_mm)
                    p.op("vector", lambda e, jj=jj: e.tensor_copy(
                        out=dsrow[jj][:], in_=p_sc[:]),
                        waits=[(s_mm, s_mm.n)], inc=s_v)

                def load_wpanel(dst, dst8, src_w, m, waits, eng):
                    src = src_w.rearrange("(dc p) m -> p dc m", p=128)[:, :, m * 128:(m + 1) * 128]
                    p.op("gpsimd", lambda e, dst8=dst8, src=src: e.dma_start(
                        out=dst8[:].rearrange("p (dc m) -> p dc m", dc=16), in_=src),
                        waits=waits, inc=s_gld, inc_n=16)
                    if eng == "scalar":
                        p.op("scalar", lambda e, dst=dst, dst8=dst8: e.copy(
                            out=dst[:], in_=dst8[:]),
                            waits=[(s_gld, s_gld.n)], inc=s_s)
                    else:
                        p.op("vector", lambda e, dst=dst, dst8=dst8: e.tensor_copy(
                            out=dst[:], in_=dst8[:]),
                            waits=[(s_gld, s_gld.n)], inc=s_v)

                gld_pan = {}
                load_wpanel(wg[0], wg8[0], gate_w, 0,
                            [(s_gld, conv_end_g)] + ffn_bar, "scalar")
                load_wpanel(wu[0], wu8[0], up_w, 0, [], "vector")
                gld_pan[0] = (s_s.n, s_v.n)
                load_wpanel(wg[1], wg8[1], gate_w, 1, [], "scalar")
                load_wpanel(wu[1], wu8[1], up_w, 1, [], "vector")
                gld_pan[1] = (s_s.n, s_v.n)

                for m in range(48):
                    j = m % 2
                    for half in range(2):
                        for dc in range(16):
                            st = (dc == 0)
                            sp = (dc == 15)
                            waits = []
                            if st and half == 0:
                                waits = [(s_s, max(gld_pan[m][0], 0)),
                                         (s_v, max(gld_pan[m][1], 0)),
                                         (s_s, s_s.n), (s_v, s_v.n)]
                            p.op("tensor", lambda e, j=j, dc=dc, st=st, sp=sp, half=half:
                                 e.matmul(p_g[half][:],
                                          wg[j][:, dc * 128:(dc + 1) * 128],
                                          xT[dc][:, half * 512:(half + 1) * 512],
                                          start=st, stop=sp), waits=waits)
                        for dc in range(16):
                            st = (dc == 0)
                            sp = (dc == 15)
                            p.op("tensor", lambda e, j=j, dc=dc, st=st, sp=sp, half=half:
                                 e.matmul(p_u[half][:],
                                          wu[j][:, dc * 128:(dc + 1) * 128],
                                          xT[dc][:, half * 512:(half + 1) * 512],
                                          start=st, stop=sp),
                                 inc=s_mm if sp and half == 1 else None)
                    mmv = s_mm.n
                    for half in range(2):
                        p.op("scalar", lambda e, m=m, half=half: e.activation(
                            out=xbf_scr[:, half * 512:(half + 1) * 512],
                            in_=p_g[half][:], func=AF.Silu, scale=scol[:, m:m + 1]),
                            waits=[(s_mm, mmv), (s_v, s_v.n)] +
                                  ([(s_ld, scol_ld)] if m == 0 and half == 0 else []),
                            inc=s_s)
                        p.op("vector", lambda e, m=m, half=half: e.scalar_tensor_tensor(
                            out=fuse[:, half * 512:(half + 1) * 512], in0=p_u[half][:],
                            scalar=scol[:, 48 + m:48 + m + 1],
                            in1=xbf_scr[:, half * 512:(half + 1) * 512],
                            op0=ALU.mult, op1=ALU.mult),
                            waits=[(s_s, s_s.n), (s_pf, s_pf.n)] +
                                  ([(s_ld, scol_ld)] if m == 0 and half == 0 else []),
                            inc=s_v)
                    p.op("sync", lambda e, m=m: e.dma_start(
                        out=h_dram[m * 128:(m + 1) * 128, :], in_=fuse[:]),
                        waits=[(s_v, s_v.n)], inc=s_pf, inc_n=16)
                    if m + 2 < 48:
                        load_wpanel(wg[j], wg8[j], gate_w, m + 2, [(s_mm, mmv)], "scalar")
                        load_wpanel(wu[j], wu8[j], up_w, m + 2, [], "vector")
                        gld_pan[m + 2] = (s_s.n, s_v.n)
                h_all_pf = s_pf.n
                if debug:
                    p.op("sync", lambda e: e.dma_start(out=h_dbg, in_=h_dram),
                         waits=[(s_pf, h_all_pf)], inc=s_st, inc_n=16)
                gateup_end_mm = s_mm.n
                gateup_end_g = s_gld.n

            with ExitStack() as bes:
                bsb = lambda name, shape, dt: bes.enter_context(nc.sbuf_tensor(name, shape, dt))
                bps = lambda name, shape, dt: bes.enter_context(nc.psum_tensor(name, shape, dt))
                wd = bsb("wd", [128, 48 * 512], BF16)
                wd8 = bsb("wd8", [128, 24 * 512], FP8)
                hpan = [bsb("hpan0", [128, 48 * 128], BF16)] * 2
                p_d = [bps(f"p_d{j}", [128, 512], F32) for j in range(2)]
                for dc in range(4):
                    for hh in range(2):
                        src = down_w.rearrange("(k p) n -> p k n", p=128)[
                            :, hh * 24:(hh + 1) * 24, dc * 512:(dc + 1) * 512]
                        p.op("gpsimd", lambda e, src=src: e.dma_start(
                            out=wd8[:].rearrange("p (k n) -> p k n", k=24), in_=src),
                            waits=[(s_mm, s_mm.n), (s_s, s_s.n), (s_v, s_v.n),
                                   (s_pf, s_pf.n)],
                            inc=s_gld, inc_n=16)
                        p.op("scalar", lambda e, hh=hh: e.copy(
                            out=wd[:, hh * 24 * 512:(hh + 1) * 24 * 512], in_=wd8[:]),
                            waits=[(s_gld, s_gld.n)], inc=s_s)
                    wd_s = s_s.n
                    for ti in range(LT):
                        jb = ti % 2
                        hsrc = h_dram.rearrange("(k p) t -> p k t", p=128)[:, :, ti * 128:(ti + 1) * 128]
                        p.op("gpsimd", lambda e, hsrc=hsrc, jb=jb: e.dma_start(
                            out=hpan[jb][:].rearrange("p (k t) -> p k t", k=48),
                            in_=hsrc),
                            waits=[(s_pf, h_all_pf), (s_mm, s_mm.n),
                                   (s_s, s_s.n), (s_v, s_v.n)],
                            inc=s_gld, inc_n=16)
                        hv = s_gld.n
                        for k in range(48):
                            st = (k == 0)
                            sp = (k == 47)
                            waits = []
                            if st:
                                waits = [(s_gld, hv), (s_v, s_v.n), (s_s, wd_s)]
                            p.op("tensor", lambda e, k=k, st=st, sp=sp, jb=jb, ti=ti:
                                 e.matmul(p_d[ti % 2][:],
                                          hpan[jb][:, k * 128:(k + 1) * 128],
                                          wd[:, k * 512:(k + 1) * 512],
                                          start=st, stop=sp),
                                 waits=waits, inc=s_mm if sp else None)
                        mmv = s_mm.n
                        p.op("vector", lambda e, dc=dc, ti=ti: e.tensor_mul(
                            out=lnscr[:, 0:512], in0=p_d[ti % 2][:], in1=dsrow[dc][:]),
                            waits=[(s_mm, mmv)], inc=s_v)
                        p.op("vector", lambda e, dc=dc, ti=ti: e.tensor_add(
                            out=stream[ti][:, dc * 512:(dc + 1) * 512],
                            in0=stream[ti][:, dc * 512:(dc + 1) * 512],
                            in1=lnscr[:, 0:512]),
                            waits=[(s_v, s_v.n)], inc=s_v)

                p_t2 = [bes.enter_context(nc.psum_tensor(f"p_t2{j}", [128, 512], BF16))
                        for j in range(2)]
                if debug:
                    for i in range(LT):
                        p.op("sync", lambda e, i=i: e.dma_start(
                            out=pre_dbg[i * 128:(i + 1) * 128, :], in_=stream[i][:]),
                            waits=[(s_v, s_v.n)], inc=s_st, inc_n=16)
                    p.wait("vector", (s_st, s_st.n))
                    p.wait("scalar", (s_st, s_st.n))
                for i in range(LT):
                    emit_layernorm(p, stream[i][:], lnscr[:], sqdump[:], col8[:],
                                   s_v, s_s, out_bf=xbf_scr[:], bf_war=(s_mm,))
                    p.op("sync", lambda e, i=i: e.dma_start(
                        out=stream_out[i * 128:(i + 1) * 128, :], in_=stream[i][:]),
                        waits=[(s_v, s_v.n)], inc=s_st, inc_n=16)
                    emit_tile_transposes(p, nc, xbf_scr[:], xT, i, id_bf[:], p_t2,
                                         s_mm, s_v, s_s, s_s.n)

        with ExitStack() as qes:
            qs = lambda name, shape, dt: qes.enter_context(nc.sbuf_tensor(name, shape, dt))
            qp = lambda name, shape, dt: qes.enter_context(nc.psum_tensor(name, shape, dt))
            wqk = [qs(f"wqk{j}", [128, D], BF16) for j in range(2)]
            ebuf = qs("ebuf", [128, 1024], BF16)
            bqrow = qs("bqrow", [1, D], BF16)
            p_q = [qp(f"p_q{j}", [128, 512], F32) for j in range(2)]
            p.op("gpsimd", lambda e: e.dma_start(out=bqrow[:], in_=bqkv[None, 0:D]),
                 waits=[(s_mm, s_mm.n), (s_v, s_v.n), (s_s, s_s.n)],
                 inc=s_gld, inc_n=16)
            bq_ld = {0: s_gld.n}

            def load_qk_panel(j, m, waits):
                src = wqkv.rearrange("(dc p) m -> p dc m", p=128)[:, :, m * 128:(m + 1) * 128]
                p.op("gpsimd", lambda e, j=j, src=src: e.dma_start(
                    out=wqk[j][:].rearrange("p (dc m) -> p dc m", dc=16), in_=src),
                    waits=waits, inc=s_gld, inc_n=16)

            qkv_bar = [(s_mm, s_mm.n), (s_v, s_v.n), (s_s, s_s.n),
                       (s_gld, s_gld.n), (s_pf, s_pf.n)]
            qk_pan = {}
            load_qk_panel(0, 0, qkv_bar)
            qk_pan[0] = s_gld.n
            load_qk_panel(1, 1, [])
            qk_pan[1] = s_gld.n
            for m in range(32):
                j = m % 2
                if m == 16:
                    p.op("gpsimd", lambda e: e.dma_start(
                        out=bqrow[:], in_=bqkv[None, D:2 * D]),
                        waits=[(s_mm, s_mm.n)], inc=s_gld, inc_n=16)
                    bq_ld[1] = s_gld.n
                bqv = bq_ld[m // 16]
                for half in range(2):
                    for dc in range(16):
                        st = (dc == 0)
                        waits = []
                        if st and half == 0:
                            waits = [(s_gld, qk_pan[m]), (s_v, s_v.n)]
                        p.op("tensor", lambda e, j=j, dc=dc, st=st, half=half:
                             e.matmul(p_q[half][:],
                                      wqk[j][:, dc * 128:(dc + 1) * 128],
                                      xT[dc][:, half * 512:(half + 1) * 512],
                                      start=st, stop=False), waits=waits)
                    mloc = m % 16
                    p.op("tensor", lambda e, mloc=mloc, half=half: e.matmul(
                        p_q[half][:], bqrow[0:1, mloc * 128:(mloc + 1) * 128],
                        ones_bf[:], start=False, stop=True),
                        waits=[(s_gld, bqv)],
                        inc=s_mm if half == 1 else None)
                mmv = s_mm.n
                for half in range(2):
                    p.op("vector", lambda e, half=half: e.tensor_copy(
                        out=ebuf[:, half * 512:(half + 1) * 512],
                        in_=p_q[half][:]),
                        waits=[(s_mm, mmv), (s_st, s_st.n)], inc=s_v)
                dst = q_out if m < 16 else k_out
                mm2 = m if m < 16 else m - 16
                p.op("sync", lambda e, dst=dst, mm2=mm2: e.dma_start(
                    out=dst[mm2 * 128:(mm2 + 1) * 128, :], in_=ebuf[:]),
                    waits=[(s_v, s_v.n)], inc=s_st, inc_n=16)
                if m + 2 < 32:
                    load_qk_panel(j, m + 2, [(s_mm, mmv)])
                    qk_pan[m + 2] = s_gld.n

            wvcol = qs("wvcol", [128, 16 * 512], BF16)
            p.op("gpsimd", lambda e: e.dma_start(
                out=bqrow[:], in_=bqkv[None, 2 * D:3 * D]),
                waits=[(s_mm, s_mm.n)], inc=s_gld, inc_n=16)
            bqv_v = s_gld.n
            for vc in range(4):
                src = wqkv.rearrange("(dc p) m -> p dc m", p=128)[:, :, 2 * D + vc * 512: 2 * D + (vc + 1) * 512]
                p.op("gpsimd", lambda e, src=src: e.dma_start(
                    out=wvcol[:].rearrange("p (dc m) -> p dc m", dc=16), in_=src),
                    waits=[(s_mm, s_mm.n)], inc=s_gld, inc_n=16)
                wv = s_gld.n
                for ti in range(LT):
                    for dc in range(16):
                        st = (dc == 0)
                        waits = []
                        if st:
                            waits = [(s_gld, wv), (s_v, s_v.n)]
                        p.op("tensor", lambda e, dc=dc, st=st, ti=ti:
                             e.matmul(p_q[ti % 2][:],
                                      xT[dc][:, ti * 128:(ti + 1) * 128],
                                      wvcol[:, dc * 512:(dc + 1) * 512],
                                      start=st, stop=False), waits=waits)
                    p.op("tensor", lambda e, vc=vc, ti=ti: e.matmul(
                        p_q[ti % 2][:], ones_bf[0:1, 0:128],
                        bqrow[0:1, vc * 512:(vc + 1) * 512],
                        start=False, stop=True),
                        waits=[(s_gld, bqv_v)] if ti == 0 else [], inc=s_mm)
                    mmv = s_mm.n
                    p.op("vector", lambda e, ti=ti: e.tensor_copy(
                        out=ebuf[:, 0:512], in_=p_q[ti % 2][:]),
                        waits=[(s_mm, mmv), (s_st, s_st.n)], inc=s_v)
                    p.op("sync", lambda e, vc=vc, ti=ti: e.dma_start(
                        out=v_out[ti * 128:(ti + 1) * 128, vc * 512:(vc + 1) * 512],
                        in_=ebuf[:, 0:512]),
                        waits=[(s_v, s_v.n)], inc=s_st, inc_n=16)

        p.wait("sync", (s_st, s_st.n))
        p.wait("gpsimd", (s_st, s_st.n))
        p.finish(blk)
    return nc


def kernel_unfused(**inputs):
    per_core, shared, B_max = prep_host(inputs)

    nc1 = build_launch1(B_max)
    in_maps = []
    for c in range(NCORES):
        m = {k: per_core[c][k] for k in ("x_own", "conv_x", "conv_w", "conv_gidx")}
        for k in ("gate_w", "up_w", "down_w", "gate_s", "up_s", "down_s",
                  "wqkv", "bqkv"):
            m[k] = shared[k]
        in_maps.append(m)
    r1 = run_spmd(nc1, in_maps)

    k_full = np.concatenate([np.asarray(r1.results[c]["k_out"]) for c in range(NCORES)], 0)
    v_full = np.concatenate([np.asarray(r1.results[c]["v_out"]) for c in range(NCORES)], 0)
    nc2 = build_launch2()
    in_maps2 = [{
        "stream_in": r1.results[c]["stream_out"], "q_in": r1.results[c]["q_out"],
        "k_full": k_full, "v_full": v_full, "wo": shared["wo"], "bo": shared["bo"],
        "router": shared["router"], "masksel": prep_masksel(c)}
        for c in range(NCORES)]
    r2 = run_spmd(nc2, in_maps2)

    x_full = np.concatenate([np.asarray(r2.results[c]["xbf_out"]) for c in range(NCORES)], 0)
    logits_full = np.concatenate([np.asarray(r2.results[c]["logit_out"]) for c in range(NCORES)], 0)
    nc3 = build_launch3()
    in_maps3 = [{
        "stream_in": r2.results[c]["stream_out"], "xbf_own": r2.results[c]["xbf_out"],
        "x_full": x_full, "logits_full": logits_full, "my_e": per_core[c]["my_e"],
        "w_ug_e": per_core[c]["w_ug_e"], "w_dn_e": per_core[c]["w_dn_e"],
        "w_sug": shared["w_sug"], "w_sd": shared["w_sd"]} for c in range(NCORES)]
    r3 = run_spmd(nc3, in_maps3)

    nc4 = build_launch4()
    in_maps4 = []
    for c in range(NCORES):
        yp = np.stack([np.asarray(r3.results[rr]["y_part"])[1024 * c:1024 * (c + 1)]
                       for rr in range(NCORES)], 0)
        in_maps4.append({"stream2_in": r3.results[c]["stream2_out"], "yparts": yp})
    r4 = run_spmd(nc4, in_maps4)
    return assemble_output([np.asarray(r4.results[c]["out_own"]) for c in range(NCORES)]).astype(np.float32)



def build_launch2():
    nc = bass.Bass("TRN2", target_bir_lowering=False)
    stream_in = nc.dram_tensor("stream_in", [LT * 128, D], F32, kind="ExternalInput").ap()
    q_in = nc.dram_tensor("q_in", [D, LT * 128], BF16, kind="ExternalInput").ap()
    k_full = nc.dram_tensor("k_full", [NCORES * D, LT * 128], BF16, kind="ExternalInput").ap()
    v_full = nc.dram_tensor("v_full", [T, D], BF16, kind="ExternalInput").ap()
    wo = nc.dram_tensor("wo", [D, D], F32, kind="ExternalInput").ap()
    bo = nc.dram_tensor("bo", [D], F32, kind="ExternalInput").ap()
    router = nc.dram_tensor("router", [D, 8], F32, kind="ExternalInput").ap()
    masksel = nc.dram_tensor("masksel", [128, NCORES * 128], BF16, kind="ExternalInput").ap()

    stream_out = nc.dram_tensor("stream_out", [LT * 128, D], F32, kind="ExternalOutput").ap()
    xbf_out = nc.dram_tensor("xbf_out", [LT * 128, D], BF16, kind="ExternalOutput").ap()
    logit_out = nc.dram_tensor("logit_out", [LT * 128, 8], F32, kind="ExternalOutput").ap()

    ALPHA = 1.0 / float(HD ** 0.5)
    id_bf_d = nc.inline_tensor(np.eye(128, dtype=BF), "id_bf").ap()
    id_f_d = nc.inline_tensor(np.eye(128, dtype=np.float32), "id_f").ap()
    ones_bf_d = nc.inline_tensor(np.ones((1, 512), dtype=BF), "ones_bf").ap()

    es = ExitStack()
    sbt = lambda name, shape, dt: es.enter_context(nc.sbuf_tensor(name, shape, dt))
    pst = lambda name, shape, dt: es.enter_context(nc.psum_tensor(name, shape, dt))

    stream = [sbt(f"stream{i}", [128, D], F32) for i in range(LT)]
    attnbuf = [sbt(f"attnbuf{i}", [128, D], BF16) for i in range(LT)]
    lnscr = sbt("lnscr", [128, D], F32)
    sqdump = sbt("sqdump", [128, D], BF16)
    col8 = sbt("col8", [128, 8], F32)
    id_bf = sbt("id_bf_s", [128, 128], BF16)
    id_f = sbt("id_f_s", [128, 128], F32)
    ones_bf = sbt("ones_bf_s", [1, 512], BF16)
    maskb = sbt("maskb", [128, NCORES * 128], BF16)
    xbf_scr = sbt("xbf_scr", [128, D], BF16)

    p = Prog(nc, es)
    s_ld = p.sem("s_ld")
    s_gld = p.sem("s_gld")
    s_mm = p.sem("s_mm")
    s_v = p.sem("s_v")
    s_s = p.sem("s_s")
    s_st = p.sem("s_st")

    blk = es.enter_context(nc.Block())
    with es:
        for i in range(LT):
            p.op("sync", lambda e, i=i: e.dma_start(
                out=stream[i][:], in_=stream_in[i * 128:(i + 1) * 128, :]),
                inc=s_ld, inc_n=16)
        p.op("sync", lambda e: e.dma_start(out=id_bf[:], in_=id_bf_d), inc=s_ld, inc_n=16)
        p.op("sync", lambda e: e.dma_start(out=id_f[:], in_=id_f_d), inc=s_ld, inc_n=16)
        p.op("sync", lambda e: e.dma_start(out=maskb[:], in_=masksel), inc=s_ld, inc_n=16)
        p.op("sync", lambda e: e.dma_start(out=ones_bf[:], in_=ones_bf_d), inc=s_ld, inc_n=16)
        ld_base = s_ld.n

        with ExitStack() as aes:
            asb = lambda name, shape, dt: aes.enter_context(nc.sbuf_tensor(name, shape, dt))
            aps = lambda name, shape, dt: aes.enter_context(nc.psum_tensor(name, shape, dt))
            kbuf = asb("kbuf", [128, T], BF16)
            vbuf = asb("vbuf", [128, T], BF16)
            qbuf = asb("qbuf", [128, LT * 128], BF16)
            scob = asb("scob", [128, T], BF16)
            ptb = asb("ptb", [128, 512], BF16)
            mc = asb("mc", [128, 8], F32)
            p_s = aps("p_s", [128, 1024], F32)
            p_av = aps("p_av", [128, HD], F32)
            p_pt = aps("p_pt", [128, 128], BF16)

            for h in range(NH):
                kv_war = [(s_mm, s_mm.n), (s_s, s_s.n)]
                for r in range(NCORES):
                    p.op("gpsimd", lambda e, h=h, r=r: e.dma_start(
                        out=kbuf[:, r * 1024:(r + 1) * 1024],
                        in_=k_full[r * D + h * HD: r * D + (h + 1) * HD, :]),
                        waits=kv_war if r == 0 else [], inc=s_gld, inc_n=16)
                p.op("gpsimd", lambda e, h=h: e.dma_start(
                    out=vbuf[:].rearrange("p (b hd) -> p b hd", b=NT),
                    in_=v_full.rearrange("(b p) hd -> p b hd", p=128)[:, :, h * HD:(h + 1) * HD]),
                    inc=s_gld, inc_n=16)
                p.op("gpsimd", lambda e, h=h: e.dma_start(
                    out=qbuf[:], in_=q_in[h * HD:(h + 1) * HD, :]),
                    inc=s_gld, inc_n=16)
                kv_ld = s_gld.n
                for i in range(LT):
                    nj = i + 1
                    W = nj * 128
                    for r in range(NCORES):
                        for piece in range(0, nj, 4):
                            w = min(4, nj - piece) * 128
                            st = (piece == 0)
                            waits = []
                            if st:
                                waits = [(s_gld, kv_ld), (s_v, s_v.n)]
                            p.op("tensor", lambda e, h=h, i=i, r=r, piece=piece, w=w:
                                 e.matmul(p_s[:, piece * 128:piece * 128 + w],
                                          qbuf[:, i * 128:(i + 1) * 128],
                                          kbuf[:, r * 1024 + piece * 128:
                                               r * 1024 + piece * 128 + w],
                                          start=True, stop=False), waits=waits)
                        p.op("tensor", lambda e, r=r, i=i: e.matmul(
                            p_s[:, i * 128:(i + 1) * 128], id_bf[:],
                            maskb[:, r * 128:(r + 1) * 128],
                            start=False, stop=True), inc=s_mm)
                        mmv = s_mm.n
                        p.op("vector", lambda e, r=r, W=W: e.tensor_copy(
                            out=scob[:, r * W:(r + 1) * W], in_=p_s[:, 0:W]),
                            waits=[(s_mm, mmv)], inc=s_v)
                    Wt = 8 * W
                    p.op("vector", lambda e, Wt=Wt: e.reduce_max(
                        out=mc[:, 0:1], in_=scob[:, 0:Wt], axis=AX),
                        waits=[(s_v, s_v.n)], inc=s_v)
                    p.op("vector", lambda e: e.tensor_scalar_mul(
                        out=mc[:, 1:2], in0=mc[:, 0:1], scalar1=-ALPHA),
                        waits=[(s_v, s_v.n)], inc=s_v)
                    p.op("scalar", lambda e, Wt=Wt: e.activation(
                        out=scob[:, 0:Wt], in_=scob[:, 0:Wt], func=AF.Exp,
                        bias=mc[:, 1:2], scale=ALPHA, accum_out=mc[:, 2:3]),
                        waits=[(s_v, s_v.n)], inc=s_s)
                    p.op("vector", lambda e: e.reciprocal(
                        out=mc[:, 3:4], in_=mc[:, 2:3]),
                        waits=[(s_s, s_s.n)], inc=s_v)
                    nt = 8 * nj
                    for mb in range(0, nt, 4):
                        group = list(range(mb, min(mb + 4, nt)))
                        cps = []
                        for g2 in group:
                            p.op("tensor", lambda e, g2=g2: e.transpose(
                                p_pt[:], scob[:, g2 * 128:(g2 + 1) * 128], id_bf[:]),
                                waits=[(s_s, s_s.n), (s_v, s_v.n)], inc=s_mm)
                            p.op("vector", lambda e, g2=g2, mb=mb: e.tensor_copy(
                                out=ptb[:, (g2 - mb) * 128:(g2 - mb + 1) * 128],
                                in_=p_pt[:]),
                                waits=[(s_mm, s_mm.n)], inc=s_v)
                        vv = s_v.n
                        for g2 in group:
                            r = g2 // nj
                            j = g2 % nj
                            b = 8 * r + j
                            st = (g2 == 0)
                            sp = (g2 == nt - 1)
                            p.op("tensor", lambda e, g2=g2, mb=mb, b=b, st=st, sp=sp:
                                 e.matmul(p_av[:],
                                          ptb[:, (g2 - mb) * 128:(g2 - mb + 1) * 128],
                                          vbuf[:, b * 128:(b + 1) * 128],
                                          start=st, stop=sp),
                                 waits=([(s_v, vv)] if g2 == group[0] else []) +
                                       ([(s_s, s_s.n)] if g2 == 0 else []),
                                 inc=s_mm if sp else None)
                    p.op("scalar", lambda e, h=h, i=i: e.activation(
                        out=attnbuf[i][:, h * HD:(h + 1) * HD], in_=p_av[:],
                        func=AF.Identity, scale=mc[:, 3:4]),
                        waits=[(s_mm, s_mm.n), (s_v, s_v.n)], inc=s_s)
            att_end = [(s_mm, s_mm.n), (s_v, s_v.n), (s_s, s_s.n),
                       (s_gld, s_gld.n)]

        with ExitStack() as pes:
            psb2 = lambda name, shape, dt: pes.enter_context(nc.sbuf_tensor(name, shape, dt))
            pps = lambda name, shape, dt: pes.enter_context(nc.psum_tensor(name, shape, dt))
            attnT = [psb2(f"attnT{cc}", [128, LT * 128], BF16) for cc in range(16)]
            wocol = psb2("wocol", [128, 16 * 512], BF16)
            borow = psb2("borow", [1, D], BF16)
            p_t2 = pps("p_t2", [128, 128], BF16)
            p_pr = [pps(f"p_pr{j}", [128, 512], F32) for j in range(2)]

            p.op("gpsimd", lambda e: e.dma_start(out=borow[:], in_=bo[None, :]),
                 waits=att_end, inc=s_gld, inc_n=16)
            bo_ld = s_gld.n
            for i in range(LT):
                emit_tile_transposes(p, nc, attnbuf[i][:], attnT, i, id_bf[:],
                                     p_t2, s_mm, s_v, s_s, 0)
            tr_done = [(s_mm, s_mm.n), (s_v, s_v.n)]
            for dc in range(4):
                p.op("gpsimd", lambda e, dc=dc: e.dma_start(
                    out=wocol[:].rearrange("p (k n) -> p k n", k=16),
                    in_=wo.rearrange("(k p) n -> p k n", p=128)[:, :, dc * 512:(dc + 1) * 512]),
                    waits=(att_end + [(s_mm, s_mm.n)]) if dc == 0 else [(s_mm, s_mm.n)],
                    inc=s_gld, inc_n=16)
                wv = s_gld.n
                for ti in range(LT):
                    for cc in range(16):
                        st = (cc == 0)
                        waits = []
                        if st:
                            waits = [(s_gld, wv), (s_v, s_v.n)] + \
                                (tr_done if ti == 0 else [])
                        p.op("tensor", lambda e, cc=cc, ti=ti, st=st:
                             e.matmul(p_pr[ti % 2][:],
                                      attnT[cc][:, ti * 128:(ti + 1) * 128],
                                      wocol[:, cc * 512:(cc + 1) * 512],
                                      start=st, stop=False), waits=waits)
                    p.op("tensor", lambda e, dc=dc, ti=ti: e.matmul(
                        p_pr[ti % 2][:], ones_bf[0:1, 0:128],
                        borow[0:1, dc * 512:(dc + 1) * 512],
                        start=False, stop=True),
                        waits=[(s_gld, bo_ld)] if (dc == 0 and ti == 0) else [],
                        inc=s_mm)
                    mmv = s_mm.n
                    p.op("vector", lambda e, dc=dc, ti=ti: e.tensor_add(
                        out=stream[ti][:, dc * 512:(dc + 1) * 512],
                        in0=p_pr[ti % 2][:],
                        in1=stream[ti][:, dc * 512:(dc + 1) * 512]),
                        waits=[(s_mm, mmv), (s_ld, ld_base)], inc=s_v)

            routs = psb2("routs", [128, 16 * 8], F32)
            xt32 = psb2("xt32", [128, 128], F32)
            lgb = psb2("lgb", [128, 8], F32)
            p_t32 = pps("p_t32", [128, 128], F32)
            p_r = pps("p_r", [128, 8], F32)
            p.op("sync", lambda e: e.dma_start(
                out=routs[:].rearrange("p (dc e) -> p dc e", dc=16),
                in_=router.rearrange("(dc p) e -> p dc e", p=128)),
                inc=s_ld, inc_n=16)
            rt_ld = s_ld.n
            for i in range(LT):
                emit_layernorm(p, stream[i][:], lnscr[:], sqdump[:], col8[:],
                               s_v, s_s, out_bf=xbf_scr[:], bf_war=(s_st,))
                sv_ln = s_v.n
                ss_ln = s_s.n
                p.op("sync", lambda e, i=i: e.dma_start(
                    out=stream_out[i * 128:(i + 1) * 128, :], in_=stream[i][:]),
                    waits=[(s_v, sv_ln)], inc=s_st, inc_n=16)
                p.op("sync", lambda e, i=i: e.dma_start(
                    out=xbf_out[i * 128:(i + 1) * 128, :], in_=xbf_scr[:]),
                    waits=[(s_s, ss_ln)], inc=s_st, inc_n=16)
                for cc in range(16):
                    p.op("tensor", lambda e, cc=cc, i=i: e.transpose(
                        p_t32[:], stream[i][:, cc * 128:(cc + 1) * 128], id_f[:]),
                        waits=[(s_v, s_v.n)], inc=s_mm)
                    p.op("vector", lambda e: e.tensor_copy(out=xt32[:], in_=p_t32[:]),
                         waits=[(s_mm, s_mm.n)], inc=s_v)
                    vv = s_v.n
                    p.op("tensor", lambda e, cc=cc: e.matmul(
                        p_r[:], xt32[:], routs[:, cc * 8:(cc + 1) * 8],
                        start=(cc == 0), stop=(cc == 15)),
                        waits=[(s_v, vv)] + ([(s_ld, rt_ld)] if cc == 0 else []),
                        inc=s_mm if cc == 15 else None)
                p.op("vector", lambda e, i=i: e.tensor_copy(
                    out=lgb[:], in_=p_r[:]),
                    waits=[(s_mm, s_mm.n), (s_st, s_st.n)], inc=s_v)
                p.op("sync", lambda e, i=i: e.dma_start(
                    out=logit_out[i * 128:(i + 1) * 128, :], in_=lgb[:]),
                    waits=[(s_v, s_v.n)], inc=s_st, inc_n=16)

        p.wait("sync", (s_st, s_st.n))
        p.wait("gpsimd", (s_st, s_st.n))
        p.finish(blk)
    return nc


def prep_masksel(c):
    m = np.zeros((128, NCORES * 128), np.float32)
    for r in range(NCORES):
        if r < c:
            m[:, r * 128:(r + 1) * 128] = 1.0
        elif r == c:
            m[:, r * 128:(r + 1) * 128] = np.triu(np.ones((128, 128), np.float32))
    return m.astype(BF)



def pi_row_block(g):
    return 1024 * (g % 8) + 128 * (g // 8)


NSC = 5
SCW = CAP // NSC


def build_launch3(debug=False):
    nc = bass.Bass("TRN2", target_bir_lowering=False)
    stream_in = nc.dram_tensor("stream_in", [LT * 128, D], F32, kind="ExternalInput").ap()
    xbf_own = nc.dram_tensor("xbf_own", [LT * 128, D], BF16, kind="ExternalInput").ap()
    x_full = nc.dram_tensor("x_full", [T, D], BF16, kind="ExternalInput").ap()
    logits_full = nc.dram_tensor("logits_full", [T, 8], F32, kind="ExternalInput").ap()
    my_e = nc.dram_tensor("my_e", [128, 8], F32, kind="ExternalInput").ap()
    w_ug_e = nc.dram_tensor("w_ug_e", [D, 2 * I_EXP], F32, kind="ExternalInput").ap()
    w_dn_e = nc.dram_tensor("w_dn_e", [I_EXP, D], F32, kind="ExternalInput").ap()
    w_sug = nc.dram_tensor("w_sug", [D, 2 * J_SH], F32, kind="ExternalInput").ap()
    w_sd = nc.dram_tensor("w_sd", [J_SH, D], F32, kind="ExternalInput").ap()

    y_part = nc.dram_tensor("y_part", [T, D], BF16, kind="ExternalOutput").ap()
    stream2_out = nc.dram_tensor("stream2_out", [LT * 128, D], F32, kind="ExternalOutput").ap()

    buf_dram = nc.dram_tensor("buf_dram", [CAP, 2056], BF16).ap()
    h2_dram = nc.dram_tensor("h2_dram", [J_SH, LT * 128], BF16).ap()
    if debug:
        h2_dbg = nc.dram_tensor("h2_dbg", [J_SH, LT * 128], BF16, kind="ExternalOutput").ap()
        xt7_dbg = nc.dram_tensor("xt7_dbg", [16 * 128, 128], BF16, kind="ExternalOutput").ap()
        selb_dbg = nc.dram_tensor("selb_dbg", [128, NT], F32, kind="ExternalOutput").ap()
        mwb_dbg = nc.dram_tensor("mwb_dbg", [128, NT], F32, kind="ExternalOutput").ap()
        base_dbg = nc.dram_tensor("base_dbg", [1, 64], F32, kind="ExternalOutput").ap()
        buf_dbg = nc.dram_tensor("buf_dbg", [CAP, 2056], BF16, kind="ExternalOutput").ap()

    id_bf_d = nc.inline_tensor(np.eye(128, dtype=BF), "id_bf").ap()
    tri128_d = nc.inline_tensor(
        np.tril(np.ones((128, 128), np.float32), -1).T.copy(), "tri128").ap()
    tri64_d = nc.inline_tensor(
        np.tril(np.ones((64, 64), np.float32), -1).T.copy(), "tri64").ap()
    ones_f_d = nc.inline_tensor(np.ones((1, 512), dtype=np.float32), "ones_f").ap()
    id64_d = nc.inline_tensor(np.eye(64, dtype=np.float32), "id64").ap()
    pi_vals = np.zeros((128, NT), np.float64)
    for g in range(NT):
        for pp2 in range(128):
            pi_vals[pp2, g] = pi_row_block(g) + pp2
    pihi = pi_vals.astype(BF)
    pilo = (pi_vals - pihi.astype(np.float64)).astype(BF)
    pihi_d = nc.inline_tensor(pihi, "pihi").ap()
    pilo_d = nc.inline_tensor(pilo, "pilo").ap()

    es = ExitStack()
    sbt = lambda name, shape, dt: es.enter_context(nc.sbuf_tensor(name, shape, dt))

    stream = [sbt(f"stream{i}", [128, D], F32) for i in range(LT)]
    xT2 = [sbt(f"xT2{cc}", [128, LT * 128], BF16) for cc in range(16)]
    id_bf = sbt("id_bf_s", [128, 128], BF16)
    ones_f = sbt("ones_f_s", [1, 512], F32)
    onescol = sbt("onescol", [128, 1], F32)

    p = Prog(nc, es)
    s_ld = p.sem("s_ld")
    s_gld = p.sem("s_gld")
    s_mm = p.sem("s_mm")
    s_v = p.sem("s_v")
    s_s = p.sem("s_s")
    s_g = p.sem("s_g")
    s_st = p.sem("s_st")
    s_fz = p.sem("s_fz")
    s_sc = p.sem("s_sc")

    blk = es.enter_context(nc.Block())
    regs = {}
    with es:
        p.q["gpsimd"].append(lambda e: regs.update(
            cap=e.to_reg(CAP - 1), tmax=e.to_reg(T - 1)))
        for i in range(LT):
            p.op("sync", lambda e, i=i: e.dma_start(
                out=stream[i][:], in_=stream_in[i * 128:(i + 1) * 128, :]),
                inc=s_ld, inc_n=16)
        p.op("sync", lambda e: e.dma_start(out=id_bf[:], in_=id_bf_d), inc=s_ld, inc_n=16)
        p.op("sync", lambda e: e.dma_start(out=ones_f[:], in_=ones_f_d), inc=s_ld, inc_n=16)
        p.op("vector", lambda e: e.memset(onescol[:], 1.0), inc=s_v)
        ld_base = s_ld.n

        with ExitStack() as tes:
            xbfl = tes.enter_context(nc.sbuf_tensor("xbfl", [128, D], BF16))
            p_t = tes.enter_context(nc.psum_tensor("p_t", [128, 128], BF16))
            for i in range(LT):
                p.op("sync", lambda e, i=i: e.dma_start(
                    out=xbfl[:], in_=xbf_own[i * 128:(i + 1) * 128, :]),
                    waits=[(s_mm, s_mm.n), (s_ld, ld_base)], inc=s_ld, inc_n=16)
                lv = s_ld.n
                for cc in range(16):
                    p.op("tensor", lambda e, cc=cc: e.transpose(
                        p_t[:], xbfl[:, cc * 128:(cc + 1) * 128], id_bf[:]),
                        waits=[(s_ld, lv), (s_v, s_v.n)], inc=s_mm)
                    p.op("vector", lambda e, cc=cc, i=i: e.tensor_copy(
                        out=xT2[cc][:, i * 128:(i + 1) * 128], in_=p_t[:]),
                        waits=[(s_mm, s_mm.n)], inc=s_v)
            xt_done = [(s_mm, s_mm.n), (s_v, s_v.n)]

        with ExitStack() as res_:
            rsb = lambda name, shape, dt: res_.enter_context(nc.sbuf_tensor(name, shape, dt))
            rps = lambda name, shape, dt: res_.enter_context(nc.psum_tensor(name, shape, dt))
            lgt = rsb("lgt", [128, 8], F32)
            mye = rsb("mye_s", [128, 8], F32)
            tri128 = rsb("tri128_s", [128, 128], F32)
            tri64 = rsb("tri64_s", [64, 64], F32)
            id64 = rsb("id64_s", [64, 64], F32)
            pihi_s = rsb("pihi_sb", [128, NT], BF16)
            pilo_s = rsb("pilo_sb", [128, NT], BF16)
            selb = rsb("selb", [128, NT], F32)
            mwb = rsb("mwb", [128, NT], F32)
            sc8 = rsb("sc8", [128, 16], F32)
            tmp8 = rsb("tmp8", [128, 8], F32)
            cntc = rsb("cntc", [64, 1], F32)
            basec = rsb("basec", [64, 1], F32)
            baserow = rsb("baserow", [1, 64], F32)
            idxi = rsb("idxi", [128, 1], I32)
            cmpt = rsb("cmpt", [128, 2056], BF16)
            fillt = rsb("fillt", [128, 2056], BF16)
            zt = rsb("zt", [128, D], BF16)
            p_r1 = rps("p_r1", [128, 128], F32)

            first = [True]
            for nm, dst, srcd in (("mye", mye, my_e), ("t128", tri128, tri128_d),
                                  ("t64", tri64, tri64_d), ("i64", id64, id64_d),
                                  ("ph", pihi_s, pihi_d), ("pl", pilo_s, pilo_d)):
                p.op("sync", lambda e, dst=dst, srcd=srcd: e.dma_start(
                    out=dst[:], in_=srcd),
                    waits=xt_done if first[0] else [], inc=s_ld, inc_n=16)
                first[0] = False
            ld2 = s_ld.n
            p.op("vector", lambda e: e.memset(fillt[:], 30000.0), inc=s_v)
            p.op("vector", lambda e: e.memset(zt[:], 0.0), inc=s_v)
            zv = s_v.n
            for sc_i in range(CAP // 128):
                p.op("sync", lambda e, sc_i=sc_i: e.dma_start(
                    out=buf_dram[sc_i * 128:(sc_i + 1) * 128, :], in_=fillt[:]),
                    waits=[(s_v, zv)] if sc_i == 0 else [], inc=s_fz, inc_n=16)
            for g in range(NT):
                p.op("sync", lambda e, g=g: e.dma_start(
                    out=y_part[g * 128:(g + 1) * 128, :], in_=zt[:]),
                    waits=[(s_v, zv)] if g == 0 else [], inc=s_fz, inc_n=16)
            fz_all = s_fz.n

            for g in range(NT):
                rb = pi_row_block(g)
                p.op("sync", lambda e, rb=rb: e.dma_start(
                    out=lgt[:], in_=logits_full[rb:rb + 128, :]),
                    waits=[(s_v, s_v.n)], inc=s_ld, inc_n=16)
                lv = s_ld.n
                p.op("vector", lambda e: e.reduce_max(out=sc8[:, 0:1], in_=lgt[:],
                                                      axis=AX),
                     waits=[(s_ld, lv)], inc=s_v)
                p.op("vector", lambda e: e.tensor_scalar(
                    out=tmp8[:], in0=lgt[:], scalar1=sc8[:, 0:1], scalar2=None,
                    op0=ALU.is_equal),
                    waits=[(s_v, s_v.n), (s_ld, ld2)], inc=s_v)
                p.op("vector", lambda e: e.scalar_tensor_tensor(
                    out=tmp8[:], in0=tmp8[:], scalar=-1e30, in1=lgt[:],
                    op0=ALU.mult, op1=ALU.add),
                    waits=[(s_v, s_v.n)], inc=s_v)
                p.op("vector", lambda e: e.reduce_max(out=sc8[:, 1:2], in_=tmp8[:],
                                                      axis=AX),
                     waits=[(s_v, s_v.n)], inc=s_v)
                p.op("vector", lambda e: e.tensor_sub(out=sc8[:, 2:3],
                                                      in0=sc8[:, 0:1],
                                                      in1=sc8[:, 1:2]),
                     waits=[(s_v, s_v.n)], inc=s_v)
                p.op("scalar", lambda e: e.activation(out=sc8[:, 3:4],
                                                      in_=sc8[:, 2:3],
                                                      func=AF.Sigmoid),
                     waits=[(s_v, s_v.n)], inc=s_s)
                p.op("scalar", lambda e: e.activation(out=sc8[:, 4:5],
                                                      in_=sc8[:, 2:3],
                                                      func=AF.Sigmoid, scale=-1.0),
                     waits=[(s_s, s_s.n)], inc=s_s)
                p.op("vector", lambda e: e.tensor_mul(out=tmp8[:], in0=lgt[:],
                                                      in1=mye[:]),
                     waits=[(s_v, s_v.n), (s_ld, ld2)], inc=s_v)
                p.op("vector", lambda e: e.reduce_sum(out=sc8[:, 5:6], in_=tmp8[:],
                                                      axis=AX),
                     waits=[(s_v, s_v.n)], inc=s_v)
                p.op("vector", lambda e: e.tensor_scalar(
                    out=sc8[:, 6:7], in0=sc8[:, 5:6], scalar1=sc8[:, 0:1],
                    scalar2=None, op0=ALU.is_equal),
                    waits=[(s_v, s_v.n)], inc=s_v)
                p.op("vector", lambda e: e.tensor_scalar(
                    out=sc8[:, 7:8], in0=sc8[:, 5:6], scalar1=sc8[:, 1:2],
                    scalar2=None, op0=ALU.is_equal),
                    waits=[(s_v, s_v.n)], inc=s_v)
                p.op("vector", lambda e: e.tensor_scalar_mul(
                    out=tmp8[:, 0:1], in0=sc8[:, 6:7], scalar1=sc8[:, 3:4]),
                    waits=[(s_v, s_v.n), (s_s, s_s.n)], inc=s_v)
                p.op("vector", lambda e: e.tensor_scalar_mul(
                    out=tmp8[:, 1:2], in0=sc8[:, 7:8], scalar1=sc8[:, 4:5]),
                    waits=[(s_v, s_v.n)], inc=s_v)
                p.op("vector", lambda e, g=g: e.tensor_add(
                    out=mwb[:, g:g + 1], in0=tmp8[:, 0:1], in1=tmp8[:, 1:2]),
                    waits=[(s_v, s_v.n)], inc=s_v)
                p.op("vector", lambda e, g=g: e.tensor_add(
                    out=selb[:, g:g + 1], in0=sc8[:, 6:7], in1=sc8[:, 7:8]),
                    waits=[(s_v, s_v.n)], inc=s_v)

            p.op("tensor", lambda e: e.matmul(
                p_r1[0:64, 0:1], selb[:], onescol[:], start=True, stop=True),
                waits=[(s_v, s_v.n)], inc=s_mm)
            p.op("vector", lambda e: e.tensor_copy(out=cntc[:], in_=p_r1[0:64, 0:1]),
                 waits=[(s_mm, s_mm.n)], inc=s_v)
            p.op("tensor", lambda e: e.matmul(
                p_r1[0:64, 0:1], tri64[:], cntc[:], start=True, stop=True),
                waits=[(s_v, s_v.n)], inc=s_mm)
            p.op("vector", lambda e: e.tensor_copy(out=basec[:], in_=p_r1[0:64, 0:1]),
                 waits=[(s_mm, s_mm.n)], inc=s_v)
            p.op("tensor", lambda e: e.matmul(
                p_r1[0:1, 0:64], basec[:], id64[:], start=True, stop=True),
                waits=[(s_v, s_v.n)], inc=s_mm)
            p.op("vector", lambda e: e.tensor_copy(out=baserow[:], in_=p_r1[0:1, 0:64]),
                 waits=[(s_mm, s_mm.n)], inc=s_v)

            for g in range(NT):
                rb = pi_row_block(g)
                p.op("tensor", lambda e, g=g: e.matmul(
                    p_r1[:, 0:1], tri128[:], selb[:, g:g + 1],
                    start=True, stop=False),
                    waits=[(s_v, s_v.n), (s_ld, ld2)])
                p.op("tensor", lambda e, g=g: e.matmul(
                    p_r1[:, 0:1], ones_f[0:1, 0:128], baserow[0:1, g:g + 1],
                    start=False, stop=True), inc=s_mm)
                mmv = s_mm.n
                p.op("vector", lambda e: e.tensor_copy(out=sc8[:, 10:11],
                                                       in_=p_r1[:, 0:1]),
                     waits=[(s_mm, mmv)], inc=s_v)
                p.op("vector", lambda e: e.tensor_scalar(
                    out=sc8[:, 11:12], in0=sc8[:, 10:11], scalar1=float(CAP),
                    scalar2=None, op0=ALU.is_lt),
                    waits=[(s_v, s_v.n)], inc=s_v)
                p.op("vector", lambda e, g=g: e.tensor_mul(
                    out=sc8[:, 12:13], in0=selb[:, g:g + 1], in1=sc8[:, 11:12]),
                    waits=[(s_v, s_v.n)], inc=s_v)
                p.op("vector", lambda e: e.tensor_mul(
                    out=sc8[:, 13:14], in0=sc8[:, 10:11], in1=sc8[:, 12:13]),
                    waits=[(s_v, s_v.n)], inc=s_v)
                p.op("vector", lambda e: e.tensor_scalar(
                    out=sc8[:, 14:15], in0=sc8[:, 12:13], scalar1=-1e9,
                    scalar2=1e9, op0=ALU.mult, op1=ALU.add),
                    waits=[(s_v, s_v.n)], inc=s_v)
                p.op("vector", lambda e: e.tensor_add(
                    out=sc8[:, 13:14], in0=sc8[:, 13:14], in1=sc8[:, 14:15]),
                    waits=[(s_v, s_v.n)], inc=s_v)
                p.op("vector", lambda e: e.tensor_copy(out=idxi[:],
                                                       in_=sc8[:, 13:14]),
                     waits=[(s_v, s_v.n), (s_sc, s_sc.n)], inc=s_v)
                p.op("vector", lambda e, g=g: e.tensor_mul(
                    out=sc8[:, 15:16], in0=mwb[:, g:g + 1], in1=sc8[:, 11:12]),
                    waits=[(s_v, s_v.n)], inc=s_v)
                p.op("scalar", lambda e: e.copy(out=cmpt[:, 2048:2049],
                                                in_=sc8[:, 15:16]),
                     waits=[(s_v, s_v.n), (s_sc, s_sc.n)], inc=s_s)
                p.op("vector", lambda e: e.tensor_sub(
                    out=cmpt[:, 2049:2050], in0=sc8[:, 15:16],
                    in1=cmpt[:, 2048:2049]),
                    waits=[(s_s, s_s.n)], inc=s_v)
                p.op("vector", lambda e, g=g: e.tensor_copy(
                    out=cmpt[:, 2050:2051], in_=pihi_s[:, g:g + 1]),
                    waits=[(s_v, s_v.n)], inc=s_v)
                p.op("vector", lambda e, g=g: e.tensor_copy(
                    out=cmpt[:, 2051:2052], in_=pilo_s[:, g:g + 1]),
                    waits=[(s_v, s_v.n)], inc=s_v)
                p.op("sync", lambda e, rb=rb: e.dma_start(
                    out=cmpt[:, 0:2048], in_=x_full[rb:rb + 128, :]),
                    waits=[(s_sc, s_sc.n)], inc=s_ld, inc_n=16)
                xv = s_ld.n
                vv = s_v.n
                p.op("gpsimd", lambda e: e.indirect_dma_start(
                    out=buf_dram, out_offset=IndirectOffsetOnAxis(idxi[:, 0:1], 0),
                    in_=cmpt[:], in_offset=None,
                    bounds_check=regs["cap"], oob_is_err=False),
                    waits=[(s_ld, xv), (s_v, vv), (s_s, s_s.n), (s_fz, fz_all)],
                    inc=s_sc, inc_n=16)
            scat_all = s_sc.n
            if debug:
                p.op("sync", lambda e: e.dma_start(out=selb_dbg, in_=selb[:]),
                     waits=[(s_v, s_v.n)], inc=s_st, inc_n=16)
                p.op("sync", lambda e: e.dma_start(out=mwb_dbg, in_=mwb[:]),
                     inc=s_st, inc_n=16)
                p.op("sync", lambda e: e.dma_start(out=base_dbg, in_=baserow[:]),
                     inc=s_st, inc_n=16)
                p.op("sync", lambda e: e.dma_start(out=buf_dbg, in_=buf_dram),
                     waits=[(s_sc, scat_all)], inc=s_st, inc_n=16)
                p.wait("vector", (s_st, s_st.n))
                p.wait("gpsimd", (s_st, s_st.n))
            rout_end = [(s_mm, s_mm.n), (s_v, s_v.n), (s_s, s_s.n),
                        (s_g, s_g.n), (s_sc, s_sc.n), (s_ld, s_ld.n)]

        with ExitStack() as ees:
            esb = lambda name, shape, dt: ees.enter_context(nc.sbuf_tensor(name, shape, dt))
            eps_ = lambda name, shape, dt: ees.enter_context(nc.psum_tensor(name, shape, dt))
            wdn_s = esb("wdn_s", [128, 11 * D], BF16)
            brow = esb("brow", [128, 2056], BF16)
            bufT = [esb(f"bufT{cc}", [128, SCW], BF16) for cc in range(16)]
            hexp = [esb(f"hexp{k}", [128, SCW], BF16) for k in range(11)]
            wpan = [esb(f"wpan{j}", [128, D], BF16) for j in range(2)]
            wsb = esb("wsb", [128, 4], F32)
            pib = esb("pib", [128, 4], I32)
            pcol = esb("pcol", [128, 1], F32)
            gsc = esb("gsc", [128, 512], BF16)
            orow = esb("orow", [128, D], BF16)
            p_t3 = eps_("p_t3", [128, 128], BF16)
            p_ge = eps_("p_ge", [128, 512], F32)
            p_ue = eps_("p_ue", [128, 512], F32)
            p_de = eps_("p_de", [128, 512], F32)

            p.op("gpsimd", lambda e: e.dma_start(
                out=wdn_s[:].rearrange("p (k n) -> p k n", k=11),
                in_=w_dn_e.rearrange("(k p) n -> p k n", p=128)),
                waits=rout_end, inc=s_gld, inc_n=16)
            wdn_ld = s_gld.n

            for sch in range(NSC):
                for sub in range(4):
                    row0 = sch * SCW + sub * 128
                    p.op("gpsimd", lambda e, row0=row0: e.dma_start(
                        out=brow[:], in_=buf_dram[row0:row0 + 128, :]),
                        waits=[(s_sc, s_sc.n), (s_mm, s_mm.n), (s_v, s_v.n),
                               (s_g, s_g.n)],
                        inc=s_gld, inc_n=16)
                    bv2 = s_gld.n
                    p.op("vector", lambda e, sub=sub: e.tensor_add(
                        out=wsb[:, sub:sub + 1], in0=brow[:, 2048:2049],
                        in1=brow[:, 2049:2050]),
                        waits=[(s_gld, bv2)], inc=s_v)
                    p.op("vector", lambda e: e.tensor_add(
                        out=pcol[:], in0=brow[:, 2050:2051],
                        in1=brow[:, 2051:2052]),
                        waits=[(s_v, s_v.n)], inc=s_v)
                    p.op("vector", lambda e, sub=sub: e.tensor_copy(
                        out=pib[:, sub:sub + 1], in_=pcol[:]),
                        waits=[(s_v, s_v.n)], inc=s_v)
                    for cc in range(16):
                        p.op("tensor", lambda e, cc=cc: e.transpose(
                            p_t3[:], brow[:, cc * 128:(cc + 1) * 128], id_bf[:]),
                            waits=[(s_gld, bv2), (s_v, s_v.n)], inc=s_mm)
                        p.op("vector", lambda e, cc=cc, sub=sub: e.tensor_copy(
                            out=bufT[cc][:, sub * 128:(sub + 1) * 128], in_=p_t3[:]),
                            waits=[(s_mm, s_mm.n)], inc=s_v)
                bt_done_v = s_v.n
                wsb_v = s_v.n

                for m in range(11):
                    j = m % 2
                    src = w_ug_e.rearrange("(dc p) m -> p dc m", p=128)[:, :, m * 128:(m + 1) * 128]
                    p.op("gpsimd", lambda e, src=src, j=j: e.dma_start(
                        out=wpan[j][:].rearrange("p (dc m) -> p dc m", dc=16),
                        in_=src),
                        waits=[(s_mm, s_mm.n), (s_v, bt_done_v)],
                        inc=s_gld, inc_n=16)
                    gv2 = s_gld.n
                    for dc in range(16):
                        st = (dc == 0)
                        sp = (dc == 15)
                        waits = [(s_gld, gv2), (s_s, s_s.n), (s_v, s_v.n)] if st else []
                        p.op("tensor", lambda e, j=j, dc=dc, st=st, sp=sp:
                             e.matmul(p_ge[:], wpan[j][:, dc * 128:(dc + 1) * 128],
                                      bufT[dc][:], start=st, stop=sp),
                             waits=waits)
                    src = w_ug_e.rearrange("(dc p) m -> p dc m", p=128)[:, :, I_EXP + m * 128: I_EXP + (m + 1) * 128]
                    p.op("gpsimd", lambda e, src=src, j=j: e.dma_start(
                        out=wpan[(j + 1) % 2][:].rearrange("p (dc m) -> p dc m", dc=16),
                        in_=src),
                        waits=[(s_mm, s_mm.n)], inc=s_gld, inc_n=16)
                    uv2 = s_gld.n
                    for dc in range(16):
                        st = (dc == 0)
                        sp = (dc == 15)
                        waits = [(s_gld, uv2)] if st else []
                        p.op("tensor", lambda e, j=j, dc=dc, st=st, sp=sp:
                             e.matmul(p_ue[:], wpan[(j + 1) % 2][:, dc * 128:(dc + 1) * 128],
                                      bufT[dc][:], start=st, stop=sp),
                             waits=waits, inc=s_mm if sp else None)
                    mmv = s_mm.n
                    p.op("scalar", lambda e: e.activation(
                        out=gsc[:], in_=p_ge[:], func=AF.Silu),
                        waits=[(s_mm, mmv), (s_v, s_v.n)], inc=s_s)
                    p.op("vector", lambda e, m=m: e.scalar_tensor_tensor(
                        out=hexp[m][:], in0=p_ue[:], scalar=1.0, in1=gsc[:],
                        op0=ALU.mult, op1=ALU.mult),
                        waits=[(s_s, s_s.n)], inc=s_v)
                gu_v = s_v.n

                for sub in range(4):
                    for dc in range(4):
                        for k in range(11):
                            st = (k == 0)
                            sp = (k == 10)
                            waits = []
                            if st:
                                waits = [(s_v, gu_v), (s_gld, wdn_ld),
                                         (s_s, s_s.n)]
                            p.op("tensor", lambda e, k=k, st=st, sp=sp, sub=sub, dc=dc:
                                 e.matmul(p_de[:],
                                          hexp[k][:, sub * 128:(sub + 1) * 128],
                                          wdn_s[:, (k * 4 + dc) * 512:(k * 4 + dc + 1) * 512],
                                          start=st, stop=sp),
                                 waits=waits, inc=s_mm if sp else None)
                        mmv = s_mm.n
                        p.op("scalar", lambda e, dc=dc, sub=sub: e.activation(
                            out=orow[:, dc * 512:(dc + 1) * 512], in_=p_de[:],
                            func=AF.Identity, scale=wsb[:, sub:sub + 1]),
                            waits=[(s_mm, mmv), (s_v, wsb_v), (s_sc, s_sc.n)],
                            inc=s_s)
                    sv2 = s_s.n
                    p.op("gpsimd", lambda e, sub=sub: e.indirect_dma_start(
                        out=y_part, out_offset=IndirectOffsetOnAxis(pib[:, sub:sub + 1], 0),
                        in_=orow[:], in_offset=None,
                        bounds_check=regs["tmax"], oob_is_err=False),
                        waits=[(s_s, sv2), (s_fz, fz_all), (s_v, s_v.n)],
                        inc=s_sc, inc_n=16)
            exp_end = [(s_mm, s_mm.n), (s_v, s_v.n), (s_s, s_s.n),
                       (s_g, s_g.n), (s_sc, s_sc.n), (s_gld, s_gld.n)]

        with ExitStack() as ses:
            ssb = lambda name, shape, dt: ses.enter_context(nc.sbuf_tensor(name, shape, dt))
            sps = lambda name, shape, dt: ses.enter_context(nc.psum_tensor(name, shape, dt))
            wpan2 = [ssb(f"wpan2{j}", [128, D], BF16) for j in range(2)]
            fuse2 = ssb("fuse2", [128, LT * 128], BF16)
            gsc2 = ssb("gsc2", [128, 512], BF16)
            p_g2 = [sps(f"p_g2{j}", [128, 512], F32) for j in range(2)]
            p_u2 = [sps(f"p_u2{j}", [128, 512], F32) for j in range(2)]

            for m in range(44):
                j = m % 2
                src = w_sug.rearrange("(dc p) m -> p dc m", p=128)[:, :, m * 128:(m + 1) * 128]
                p.op("gpsimd", lambda e, src=src, j=j: e.dma_start(
                    out=wpan2[j][:].rearrange("p (dc m) -> p dc m", dc=16), in_=src),
                    waits=(exp_end if m == 0 else [(s_mm, s_mm.n)]),
                    inc=s_gld, inc_n=16)
                gv3 = s_gld.n
                for half in range(2):
                    for dc in range(16):
                        st = (dc == 0)
                        sp = (dc == 15)
                        waits = [(s_gld, gv3), (s_s, s_s.n), (s_v, s_v.n)] if (st and half == 0) else []
                        p.op("tensor", lambda e, j=j, dc=dc, st=st, sp=sp, half=half:
                             e.matmul(p_g2[half][:],
                                      wpan2[j][:, dc * 128:(dc + 1) * 128],
                                      xT2[dc][:, half * 512:(half + 1) * 512],
                                      start=st, stop=sp), waits=waits)
                src = w_sug.rearrange("(dc p) m -> p dc m", p=128)[:, :, J_SH + m * 128: J_SH + (m + 1) * 128]
                p.op("gpsimd", lambda e, src=src, j=j: e.dma_start(
                    out=wpan2[(j + 1) % 2][:].rearrange("p (dc m) -> p dc m", dc=16),
                    in_=src),
                    waits=[(s_mm, s_mm.n)], inc=s_gld, inc_n=16)
                uv3 = s_gld.n
                for half in range(2):
                    for dc in range(16):
                        st = (dc == 0)
                        sp = (dc == 15)
                        waits = [(s_gld, uv3)] if (st and half == 0) else []
                        p.op("tensor", lambda e, j=j, dc=dc, st=st, sp=sp, half=half:
                             e.matmul(p_u2[half][:],
                                      wpan2[(j + 1) % 2][:, dc * 128:(dc + 1) * 128],
                                      xT2[dc][:, half * 512:(half + 1) * 512],
                                      start=st, stop=sp),
                             waits=waits, inc=s_mm if (sp and half == 1) else None)
                mmv = s_mm.n
                for half in range(2):
                    p.op("scalar", lambda e, half=half: e.activation(
                        out=gsc2[:], in_=p_g2[half][:], func=AF.Silu),
                        waits=[(s_mm, mmv), (s_v, s_v.n)], inc=s_s)
                    p.op("vector", lambda e, half=half: e.scalar_tensor_tensor(
                        out=fuse2[:, half * 512:(half + 1) * 512],
                        in0=p_u2[half][:], scalar=1.0, in1=gsc2[:],
                        op0=ALU.mult, op1=ALU.mult),
                        waits=[(s_s, s_s.n), (s_fz, s_fz.n)], inc=s_v)
                p.op("sync", lambda e, m=m: e.dma_start(
                    out=h2_dram[m * 128:(m + 1) * 128, :], in_=fuse2[:]),
                    waits=[(s_v, s_v.n)], inc=s_fz, inc_n=16)
            h2_all = s_fz.n
            if debug:
                p.op("sync", lambda e: e.dma_start(out=h2_dbg, in_=h2_dram),
                     waits=[(s_fz, h2_all)], inc=s_st, inc_n=16)
                for cc in range(16):
                    p.op("sync", lambda e, cc=cc: e.dma_start(
                        out=xt7_dbg[cc * 128:(cc + 1) * 128, :],
                        in_=xT2[cc][:, 7 * 128:8 * 128]),
                        inc=s_st, inc_n=16)

            wsd_s = ssb("wsd_s", [128, 44 * 512], BF16)
            hpan2 = ssb("hpan2", [128, 44 * 128], BF16)
            p_d2 = [sps(f"p_d2{j}", [128, 512], F32) for j in range(2)]
            for dc in range(4):
                src = w_sd.rearrange("(k p) n -> p k n", p=128)[:, :, dc * 512:(dc + 1) * 512]
                p.op("gpsimd", lambda e, src=src: e.dma_start(
                    out=wsd_s[:].rearrange("p (k n) -> p k n", k=44), in_=src),
                    waits=[(s_mm, s_mm.n), (s_s, s_s.n), (s_v, s_v.n)],
                    inc=s_gld, inc_n=16)
                wv3 = s_gld.n
                for ti in range(LT):
                    hsrc = h2_dram.rearrange("(k p) t -> p k t", p=128)[:, :, ti * 128:(ti + 1) * 128]
                    p.op("gpsimd", lambda e, hsrc=hsrc: e.dma_start(
                        out=hpan2[:].rearrange("p (k t) -> p k t", k=44), in_=hsrc),
                        waits=[(s_fz, h2_all), (s_mm, s_mm.n)],
                        inc=s_gld, inc_n=16)
                    hv3 = s_gld.n
                    for k in range(44):
                        st = (k == 0)
                        sp = (k == 43)
                        waits = [(s_gld, hv3), (s_v, s_v.n)] if st else []
                        p.op("tensor", lambda e, k=k, st=st, sp=sp, ti=ti:
                             e.matmul(p_d2[ti % 2][:],
                                      hpan2[:, k * 128:(k + 1) * 128],
                                      wsd_s[:, k * 512:(k + 1) * 512],
                                      start=st, stop=sp),
                             waits=waits, inc=s_mm if sp else None)
                    mmv = s_mm.n
                    p.op("vector", lambda e, dc=dc, ti=ti: e.tensor_add(
                        out=stream[ti][:, dc * 512:(dc + 1) * 512],
                        in0=p_d2[ti % 2][:],
                        in1=stream[ti][:, dc * 512:(dc + 1) * 512]),
                        waits=[(s_mm, mmv), (s_ld, ld_base)], inc=s_v)
            for i in range(LT):
                p.op("sync", lambda e, i=i: e.dma_start(
                    out=stream2_out[i * 128:(i + 1) * 128, :], in_=stream[i][:]),
                    waits=[(s_v, s_v.n)], inc=s_st, inc_n=16)

        p.wait("sync", (s_st, s_st.n), (s_sc, s_sc.n), (s_fz, s_fz.n))
        p.wait("gpsimd", (s_st, s_st.n))
        p.finish(blk)
    return nc



def build_launch4():
    nc = bass.Bass("TRN2", target_bir_lowering=False)
    stream2_in = nc.dram_tensor("stream2_in", [LT * 128, D], F32, kind="ExternalInput").ap()
    yparts = nc.dram_tensor("yparts", [NCORES, LT * 128, D], BF16, kind="ExternalInput").ap()
    out_own = nc.dram_tensor("out_own", [LT * 128, D], F32, kind="ExternalOutput").ap()

    es = ExitStack()
    sbt = lambda name, shape, dt: es.enter_context(nc.sbuf_tensor(name, shape, dt))
    st_t = sbt("st_t", [128, D], F32)
    yb = [sbt(f"yb{j}", [128, D], BF16) for j in range(2)]
    lnscr = sbt("lnscr", [128, D], F32)
    sqdump = sbt("sqdump", [128, D], BF16)
    col8 = sbt("col8", [128, 8], F32)

    p = Prog(nc, es)
    s_ld = p.sem("s_ld")
    s_v = p.sem("s_v")
    s_s = p.sem("s_s")
    s_st = p.sem("s_st")

    blk = es.enter_context(nc.Block())
    with es:
        for i in range(LT):
            p.op("sync", lambda e, i=i: e.dma_start(
                out=st_t[:], in_=stream2_in[i * 128:(i + 1) * 128, :]),
                waits=[(s_st, s_st.n), (s_v, s_v.n)], inc=s_ld, inc_n=16)
            for r in range(NCORES):
                p.op("sync", lambda e, i=i, r=r: e.dma_start(
                    out=yb[r % 2][:], in_=yparts[r, i * 128:(i + 1) * 128, :]),
                    waits=[(s_v, s_v.n)], inc=s_ld, inc_n=16)
                lv = s_ld.n
                p.op("vector", lambda e, r=r: e.tensor_add(
                    out=st_t[:], in0=st_t[:], in1=yb[r % 2][:]),
                    waits=[(s_ld, lv), (s_v, s_v.n)], inc=s_v)
            emit_layernorm(p, st_t[:], lnscr[:], sqdump[:], col8[:], s_v, s_s)
            p.op("sync", lambda e, i=i: e.dma_start(
                out=out_own[i * 128:(i + 1) * 128, :], in_=st_t[:]),
                waits=[(s_v, s_v.n), (s_s, s_s.n)], inc=s_st, inc_n=16)
        p.wait("sync", (s_st, s_st.n))
        p.finish(blk)
    return nc




def build_fused(B_max, debug=False):
    dbg_routs_holder = []
    nc = bass.Bass("TRN2", target_bir_lowering=False)
    x_own = nc.dram_tensor("x_own", [LT * 128, D], F32, kind="ExternalInput").ap()
    conv_x = nc.dram_tensor("conv_x", [B_max, 64, D], F32, kind="ExternalInput").ap()
    conv_w = nc.dram_tensor("conv_w", [B_max, 16, 64, 64], F32, kind="ExternalInput").ap()
    conv_gidx = nc.dram_tensor("conv_gidx", [128, 2 * LT], I32, kind="ExternalInput").ap()
    gate_w = nc.dram_tensor("gate_w", [D, I_FFN], F32, kind="ExternalInput").ap()
    up_w = nc.dram_tensor("up_w", [D, I_FFN], F32, kind="ExternalInput").ap()
    down_w = nc.dram_tensor("down_w", [I_FFN, D], F32, kind="ExternalInput").ap()
    gate_s = nc.dram_tensor("gate_s", [I_FFN], F32, kind="ExternalInput").ap()
    up_s = nc.dram_tensor("up_s", [I_FFN], F32, kind="ExternalInput").ap()
    down_s = nc.dram_tensor("down_s", [D], F32, kind="ExternalInput").ap()
    scol_in = nc.dram_tensor("scol_in", [128, 96], F32, kind="ExternalInput").ap()
    wqkv = nc.dram_tensor("wqkv", [D, 3 * D], F32, kind="ExternalInput").ap()
    bqkv = nc.dram_tensor("bqkv", [3 * D], F32, kind="ExternalInput").ap()

    q_out = nc.dram_tensor("q_dram", [D, LT * 128], BF16).ap()
    k_out = nc.dram_tensor("kag_in", [D, LT * 128], BF16).ap()
    v_out = nc.dram_tensor("vag_in", [LT * 128, D], BF16).ap()

    c_dram = nc.dram_tensor("c_dram", [(B_max + 1) * 64, D], F32).ap()
    h_dram = nc.dram_tensor("h_dram", [I_FFN, LT * 128], BF16).ap()

    id_bf_d = nc.inline_tensor(np.eye(128, dtype=BF), "id_bf").ap()
    ones_bf_d = nc.inline_tensor(np.ones((1, 512), dtype=BF), "ones_bf").ap()
    ones_f_d = nc.inline_tensor(np.ones((1, 512), dtype=np.float32), "ones_f").ap()

    es = ExitStack()
    sbt = lambda name, shape, dt: es.enter_context(nc.sbuf_tensor(name, shape, dt))

    stream = [sbt(f"stream{i}", [128, D], F32) for i in range(LT)]
    xT = [sbt(f"xT{c}", [128, LT * 128], BF16) for c in range(16)]
    xbf_scr = sbt("xbf_scr", [128, D], BF16)
    lnscr = sbt("lnscr", [128, D], F32)
    sqdump = sbt("sqdump", [128, D], BF16)
    col8 = sbt("col8", [128, 8], F32)
    id_bf = sbt("id_bf_s", [128, 128], BF16)
    ones_bf = sbt("ones_bf_s", [1, 512], BF16)
    ones_f = sbt("ones_f_s", [1, 512], F32)
    gidx_s = sbt("gidx_s", [128, 2 * LT], I32)

    p = Prog(nc, es)
    s_ld = p.sem("s_ld")
    s_gld = p.sem("s_gld")
    s_mm = p.sem("s_mm")
    s_v = p.sem("s_v")
    s_s = p.sem("s_s")
    s_st = p.sem("s_st")
    s_pf = p.sem("s_pf")
    s_fz = p.sem("s_fz")
    s_sc = p.sem("s_sc")
    s_g = p.sem("s_g")
    s_col = p.sem("s_col")
    _all_sems = [s_ld, s_gld, s_mm, s_v, s_s, s_st, s_pf, s_fz, s_sc, s_g, s_col]

    def _fence():
        w = [(s, s.n) for s in _all_sems]
        for _eng in Prog.ENGS:
            p.wait(_eng, *w)

    blk = es.enter_context(nc.Block())
    with es:
        for i in range(LT):
            p.op("sync", lambda e, i=i: e.dma_start(
                out=stream[i][:], in_=x_own[i * 128:(i + 1) * 128, :]),
                inc=s_ld, inc_n=16)
        p.op("sync", lambda e: e.dma_start(out=id_bf[:], in_=id_bf_d), inc=s_ld, inc_n=16)
        p.op("sync", lambda e: e.dma_start(out=ones_bf[:], in_=ones_bf_d), inc=s_ld, inc_n=16)
        p.op("sync", lambda e: e.dma_start(out=ones_f[:], in_=ones_f_d), inc=s_ld, inc_n=16)
        p.op("sync", lambda e: e.dma_start(out=gidx_s[:], in_=conv_gidx),
             inc=s_ld, inc_n=16)
        ld_base = s_ld.n

        with ExitStack() as ces:
            cs = lambda name, shape, dt: ces.enter_context(nc.sbuf_tensor(name, shape, dt))
            cxs = [cs(f"cxs{j}", [64, 16 + D], BF16) for j in range(2)]
            cws = [cs(f"cws{j}", [64, 16 * 64], BF16) for j in range(2)]
            cpref = [cs(f"cpref{j}", [64, D], F32) for j in range(2)]
            gbufA = cs("gbufA", [128, D], F32)
            gbufB = cs("gbufB", [128, D], F32)
            p_conv = ces.enter_context(nc.psum_tensor("p_conv", [64, D], F32))
            p_t = [ces.enter_context(nc.psum_tensor(f"p_t{j}", [128, 512], BF16))
                   for j in range(2)]

            p.op("vector", lambda e: e.memset(cpref[1][:], 0.0), inc=s_v)
            zv = s_v.n
            p.op("vector", lambda e: e.memset(cxs[0][:, 0:16], 0.0), inc=s_v)
            p.op("vector", lambda e: e.memset(cxs[1][:, 0:16], 0.0), inc=s_v)
            zx = s_v.n
            p.op("sync", lambda e: e.dma_start(out=c_dram[0:64, :], in_=cpref[1][:]),
                 waits=[(s_v, zv)], inc=s_pf, inc_n=16)

            conv_mm_done = {}
            conv_pref_done = {}
            gld0 = s_gld.n

            def load_slot(s):
                j = s % 2
                wv = conv_mm_done.get(s - 2, 0)
                p.op("gpsimd", lambda e, s=s, j=j: e.dma_start(
                    out=cxs[j][:, 16:], in_=conv_x[s]),
                    waits=[(s_mm, wv), (s_v, zx)], inc=s_gld, inc_n=16)
                p.op("gpsimd", lambda e, s=s, j=j: e.dma_start(
                    out=cws[j][:].rearrange("j (k i) -> j k i", k=16),
                    in_=conv_w[s].rearrange("k j i -> j k i")),
                    inc=s_gld, inc_n=16)

            load_slot(0)
            if B_max > 1:
                load_slot(1)
            for s in range(B_max):
                j = s % 2
                ldv = gld0 + 32 * min(s + 2, B_max)
                for ch in range(4):
                    for k in range(16):
                        lhsT = cws[j][:, k * 64:(k + 1) * 64]
                        rhs = cxs[j][:, 1 + k + ch * 512: 1 + k + ch * 512 + 512]
                        st = (k == 0)
                        sp = (k == 15)
                        waits = []
                        if ch == 0 and k == 0:
                            waits = [(s_gld, ldv),
                                     (s_v, conv_pref_done.get(s - 1, 0))]
                        last = (sp and ch == 3)
                        p.op("tensor", lambda e, lhsT=lhsT, rhs=rhs, st=st, sp=sp, ch=ch:
                             e.matmul(p_conv[:, ch * 512:(ch + 1) * 512], lhsT,
                                      rhs, start=st, stop=sp),
                             waits=waits, inc=s_mm if last else None)
                conv_mm_done[s] = s_mm.n
                prev = cpref[(s + 1) % 2]
                cur = cpref[s % 2]
                for ch in range(4):
                    waits = []
                    if ch == 0:
                        waits = [(s_mm, conv_mm_done[s]), (s_pf, 16 * s)]
                    if s == 0:
                        p.op("vector", lambda e, cur=cur, ch=ch: e.tensor_copy(
                            out=cur[:, ch * 512:(ch + 1) * 512],
                            in_=p_conv[:, ch * 512:(ch + 1) * 512]),
                            waits=waits, inc=s_v)
                    else:
                        p.op("vector", lambda e, cur=cur, prev=prev, ch=ch:
                             e.tensor_add(
                                 out=cur[:, ch * 512:(ch + 1) * 512],
                                 in0=p_conv[:, ch * 512:(ch + 1) * 512],
                                 in1=prev[:, ch * 512:(ch + 1) * 512]),
                             waits=waits, inc=s_v)
                conv_pref_done[s] = s_v.n
                p.op("sync", lambda e, s=s, cur=cur: e.dma_start(
                    out=c_dram[(s + 1) * 64:(s + 2) * 64, :], in_=cur[:]),
                    waits=[(s_v, conv_pref_done[s])], inc=s_pf, inc_n=16)
                if s + 2 < B_max:
                    load_slot(s + 2)

            pf_all = s_pf.n
            for i in range(LT):
                p.op("gpsimd", lambda e, i=i: e.indirect_dma_start(
                    out=gbufA[:], out_offset=None, in_=c_dram,
                    in_offset=IndirectOffsetOnAxis(gidx_s[:, i:i + 1], 0),
                    bounds_check=(B_max + 1) * 64 - 1, oob_is_err=True),
                    waits=[(s_pf, pf_all), (s_ld, ld_base), (s_v, s_v.n)],
                    inc=s_gld, inc_n=16)
                p.op("gpsimd", lambda e, i=i: e.indirect_dma_start(
                    out=gbufB[:], out_offset=None, in_=c_dram,
                    in_offset=IndirectOffsetOnAxis(gidx_s[:, LT + i:LT + i + 1], 0),
                    bounds_check=(B_max + 1) * 64 - 1, oob_is_err=True),
                    inc=s_gld, inc_n=16)
                gv = s_gld.n
                p.op("vector", lambda e: e.tensor_sub(out=gbufA[:], in0=gbufA[:],
                                                      in1=gbufB[:]),
                     waits=[(s_gld, gv)], inc=s_v)
                p.wait("vector", (s_v, s_v.n))
                p.op("vector", lambda e, i=i: e.tensor_add(
                    out=stream[i][:], in0=stream[i][:], in1=gbufA[:]),
                    waits=[], inc=s_v)
                emit_layernorm(p, stream[i][:], lnscr[:], sqdump[:], col8[:],
                               s_v, s_s, out_bf=xbf_scr[:], bf_war=(s_mm,))
                emit_tile_transposes(p, nc, xbf_scr[:], xT, i, id_bf[:], p_t,
                                     s_mm, s_v, s_s, s_s.n)
            conv_end_mm = s_mm.n
            conv_end_g = s_gld.n

        with ExitStack() as fes:
            fs = lambda name, shape, dt: fes.enter_context(nc.sbuf_tensor(name, shape, dt))
            fp = lambda name, shape, dt: fes.enter_context(nc.psum_tensor(name, shape, dt))
            scol = fs("scol", [128, 96], F32)
            dsrow = [fs(f"dsrow{j}", [128, 512], F32) for j in range(4)]

            with ExitStack() as aes:
                asb = lambda name, shape, dt: aes.enter_context(nc.sbuf_tensor(name, shape, dt))
                aps = lambda name, shape, dt: aes.enter_context(nc.psum_tensor(name, shape, dt))
                wg = [asb(f"wg{j}", [128, D], BF16) for j in range(2)]
                wu = [asb(f"wu{j}", [128, D], BF16) for j in range(2)]
                wg8 = [asb(f"wg8{j}", [128, D], FP8) for j in range(2)]
                wu8 = [asb(f"wu8{j}", [128, D], FP8) for j in range(2)]
                fuse = asb("fuse", [128, LT * 128], BF16)
                srow = asb("srow", [1, D], F32)
                p_g = [aps(f"p_g{j}", [128, 512], F32) for j in range(2)]
                p_u = [aps(f"p_u{j}", [128, 512], F32) for j in range(2)]
                p_sc = aps("p_sc", [128, 512], F32)

                ffn_bar = [(s_mm, s_mm.n), (s_v, s_v.n), (s_s, s_s.n)]
                p.op("sync", lambda e: e.dma_start(
                    out=scol[:, 0:96], in_=scol_in),
                    waits=list(ffn_bar), inc=s_ld, inc_n=16)
                scol_ld = s_ld.n
                p.op("sync", lambda e: e.dma_start(out=srow[:], in_=down_s[None, :]),
                     waits=[(s_mm, s_mm.n)], inc=s_ld, inc_n=16)
                dsl = s_ld.n
                for jj in range(4):
                    p.op("tensor", lambda e, jj=jj: e.matmul(
                        p_sc[:], ones_f[0:1, 0:128],
                        srow[0:1, jj * 512:(jj + 1) * 512], start=True, stop=True),
                        waits=[(s_ld, dsl), (s_v, s_v.n)], inc=s_mm)
                    p.op("vector", lambda e, jj=jj: e.tensor_copy(
                        out=dsrow[jj][:], in_=p_sc[:]),
                        waits=[(s_mm, s_mm.n)], inc=s_v)

                def load_wpanel(dst, dst8, src_w, m, waits, eng):
                    src = src_w.rearrange("(dc p) m -> p dc m", p=128)[:, :, m * 128:(m + 1) * 128]
                    p.op("gpsimd", lambda e, dst8=dst8, src=src: e.dma_start(
                        out=dst8[:].rearrange("p (dc m) -> p dc m", dc=16), in_=src),
                        waits=waits, inc=s_gld, inc_n=16)
                    if eng == "scalar":
                        p.op("scalar", lambda e, dst=dst, dst8=dst8: e.copy(
                            out=dst[:], in_=dst8[:]),
                            waits=[(s_gld, s_gld.n)], inc=s_s)
                    else:
                        p.op("vector", lambda e, dst=dst, dst8=dst8: e.tensor_copy(
                            out=dst[:], in_=dst8[:]),
                            waits=[(s_gld, s_gld.n)], inc=s_v)

                gld_pan = {}
                load_wpanel(wg[0], wg8[0], gate_w, 0,
                            [(s_gld, conv_end_g)] + ffn_bar, "scalar")
                load_wpanel(wu[0], wu8[0], up_w, 0, [], "vector")
                gld_pan[0] = (s_s.n, s_v.n)
                load_wpanel(wg[1], wg8[1], gate_w, 1, [], "scalar")
                load_wpanel(wu[1], wu8[1], up_w, 1, [], "vector")
                gld_pan[1] = (s_s.n, s_v.n)

                for m in range(48):
                    j = m % 2
                    for half in range(2):
                        for dc in range(16):
                            st = (dc == 0)
                            sp = (dc == 15)
                            waits = []
                            if st and half == 0:
                                waits = [(s_s, max(gld_pan[m][0], 0)),
                                         (s_v, max(gld_pan[m][1], 0)),
                                         (s_s, s_s.n), (s_v, s_v.n)]
                            p.op("tensor", lambda e, j=j, dc=dc, st=st, sp=sp, half=half:
                                 e.matmul(p_g[half][:],
                                          wg[j][:, dc * 128:(dc + 1) * 128],
                                          xT[dc][:, half * 512:(half + 1) * 512],
                                          start=st, stop=sp), waits=waits)
                        for dc in range(16):
                            st = (dc == 0)
                            sp = (dc == 15)
                            p.op("tensor", lambda e, j=j, dc=dc, st=st, sp=sp, half=half:
                                 e.matmul(p_u[half][:],
                                          wu[j][:, dc * 128:(dc + 1) * 128],
                                          xT[dc][:, half * 512:(half + 1) * 512],
                                          start=st, stop=sp),
                                 inc=s_mm if sp and half == 1 else None)
                    mmv = s_mm.n
                    for half in range(2):
                        p.op("scalar", lambda e, m=m, half=half: e.activation(
                            out=xbf_scr[:, half * 512:(half + 1) * 512],
                            in_=p_g[half][:], func=AF.Silu, scale=scol[:, m:m + 1]),
                            waits=[(s_mm, mmv), (s_v, s_v.n)] +
                                  ([(s_ld, scol_ld)] if m == 0 and half == 0 else []),
                            inc=s_s)
                        p.op("vector", lambda e, m=m, half=half: e.scalar_tensor_tensor(
                            out=fuse[:, half * 512:(half + 1) * 512], in0=p_u[half][:],
                            scalar=scol[:, 48 + m:48 + m + 1],
                            in1=xbf_scr[:, half * 512:(half + 1) * 512],
                            op0=ALU.mult, op1=ALU.mult),
                            waits=[(s_s, s_s.n), (s_pf, s_pf.n)] +
                                  ([(s_ld, scol_ld)] if m == 0 and half == 0 else []),
                            inc=s_v)
                    p.op("sync", lambda e, m=m: e.dma_start(
                        out=h_dram[m * 128:(m + 1) * 128, :], in_=fuse[:]),
                        waits=[(s_v, s_v.n)], inc=s_pf, inc_n=16)
                    if m + 2 < 48:
                        load_wpanel(wg[j], wg8[j], gate_w, m + 2, [(s_mm, mmv)], "scalar")
                        load_wpanel(wu[j], wu8[j], up_w, m + 2, [], "vector")
                        gld_pan[m + 2] = (s_s.n, s_v.n)
                h_all_pf = s_pf.n
                gateup_end_mm = s_mm.n
                gateup_end_g = s_gld.n

            with ExitStack() as bes:
                bsb = lambda name, shape, dt: bes.enter_context(nc.sbuf_tensor(name, shape, dt))
                bps = lambda name, shape, dt: bes.enter_context(nc.psum_tensor(name, shape, dt))
                wd = bsb("wd", [128, 48 * 512], BF16)
                wd8 = bsb("wd8", [128, 24 * 512], FP8)
                hpan = [bsb("hpan0", [128, 48 * 128], BF16)] * 2
                p_d = [bps(f"p_d{j}", [128, 512], F32) for j in range(2)]
                for dc in range(4):
                    for hh in range(2):
                        src = down_w.rearrange("(k p) n -> p k n", p=128)[
                            :, hh * 24:(hh + 1) * 24, dc * 512:(dc + 1) * 512]
                        p.op("gpsimd", lambda e, src=src: e.dma_start(
                            out=wd8[:].rearrange("p (k n) -> p k n", k=24), in_=src),
                            waits=[(s_mm, s_mm.n), (s_s, s_s.n), (s_v, s_v.n),
                                   (s_pf, s_pf.n)],
                            inc=s_gld, inc_n=16)
                        p.op("scalar", lambda e, hh=hh: e.copy(
                            out=wd[:, hh * 24 * 512:(hh + 1) * 24 * 512], in_=wd8[:]),
                            waits=[(s_gld, s_gld.n)], inc=s_s)
                    wd_s = s_s.n
                    for ti in range(LT):
                        jb = ti % 2
                        hsrc = h_dram.rearrange("(k p) t -> p k t", p=128)[:, :, ti * 128:(ti + 1) * 128]
                        p.op("gpsimd", lambda e, hsrc=hsrc, jb=jb: e.dma_start(
                            out=hpan[jb][:].rearrange("p (k t) -> p k t", k=48),
                            in_=hsrc),
                            waits=[(s_pf, h_all_pf), (s_mm, s_mm.n),
                                   (s_s, s_s.n), (s_v, s_v.n)],
                            inc=s_gld, inc_n=16)
                        hv = s_gld.n
                        for k in range(48):
                            st = (k == 0)
                            sp = (k == 47)
                            waits = []
                            if st:
                                waits = [(s_gld, hv), (s_v, s_v.n), (s_s, wd_s)]
                            p.op("tensor", lambda e, k=k, st=st, sp=sp, jb=jb, ti=ti:
                                 e.matmul(p_d[ti % 2][:],
                                          hpan[jb][:, k * 128:(k + 1) * 128],
                                          wd[:, k * 512:(k + 1) * 512],
                                          start=st, stop=sp),
                                 waits=waits, inc=s_mm if sp else None)
                        mmv = s_mm.n
                        p.op("vector", lambda e, dc=dc, ti=ti: e.tensor_mul(
                            out=lnscr[:, 0:512], in0=p_d[ti % 2][:], in1=dsrow[dc][:]),
                            waits=[(s_mm, mmv)], inc=s_v)
                        p.op("vector", lambda e, dc=dc, ti=ti: e.tensor_add(
                            out=stream[ti][:, dc * 512:(dc + 1) * 512],
                            in0=stream[ti][:, dc * 512:(dc + 1) * 512],
                            in1=lnscr[:, 0:512]),
                            waits=[(s_v, s_v.n)], inc=s_v)

                p_t2 = [bes.enter_context(nc.psum_tensor(f"p_t2{j}", [128, 512], BF16))
                        for j in range(2)]
                for i in range(LT):
                    emit_layernorm(p, stream[i][:], lnscr[:], sqdump[:], col8[:],
                                   s_v, s_s, out_bf=xbf_scr[:], bf_war=(s_mm,))
                    emit_tile_transposes(p, nc, xbf_scr[:], xT, i, id_bf[:], p_t2,
                                         s_mm, s_v, s_s, s_s.n)

        with ExitStack() as qes:
            qs = lambda name, shape, dt: qes.enter_context(nc.sbuf_tensor(name, shape, dt))
            qp = lambda name, shape, dt: qes.enter_context(nc.psum_tensor(name, shape, dt))
            wqk = [qs(f"wqk{j}", [128, D], BF16) for j in range(2)]
            ebuf = qs("ebuf", [128, 1024], BF16)
            bqrow = qs("bqrow", [1, D], BF16)
            p_q = [qp(f"p_q{j}", [128, 512], F32) for j in range(2)]
            p.op("gpsimd", lambda e: e.dma_start(out=bqrow[:], in_=bqkv[None, D:2 * D]),
                 waits=[(s_mm, s_mm.n), (s_v, s_v.n), (s_s, s_s.n)],
                 inc=s_gld, inc_n=16)
            bq_ld = {0: (s_gld, s_gld.n)}
            qk_pan = {}

            def load_qk_panel(j, m, waits, eng="gpsimd", sem=None):
                sem = sem if sem is not None else s_gld
                src = wqkv.rearrange("(dc p) m -> p dc m", p=128)[:, :, m * 128:(m + 1) * 128]
                p.op(eng, lambda e, j=j, src=src: e.dma_start(
                    out=wqk[j][:].rearrange("p (dc m) -> p dc m", dc=16), in_=src),
                    waits=waits, inc=sem, inc_n=16)
                qk_pan[m] = (sem, sem.n)

            qkv_bar = [(s_mm, s_mm.n), (s_v, s_v.n), (s_s, s_s.n),
                       (s_gld, s_gld.n), (s_pf, s_pf.n)]
            morder = list(range(16, 32)) + list(range(16))
            load_qk_panel(0, morder[0], qkv_bar)
            load_qk_panel(1, morder[1], [])

            def qk_iter(mi, allbuf=False):
                m = morder[mi]
                j = mi % 2
                bqv = bq_ld[mi // 16]
                for half in range(2):
                    for dc in range(16):
                        st = (dc == 0)
                        waits = []
                        if st and half == 0:
                            waits = [qk_pan[m], (s_v, s_v.n)]
                        if allbuf:
                            p.op("tensor", lambda e, m=m, dc=dc, st=st, half=half:
                                 e.matmul(p_q[half][:],
                                          wqk_all[:, m * D + dc * 128:
                                                  m * D + dc * 128 + 128],
                                          xT[dc][:, half * 512:(half + 1) * 512],
                                          start=st, stop=False), waits=waits)
                        else:
                            p.op("tensor", lambda e, j=j, dc=dc, st=st, half=half:
                                 e.matmul(p_q[half][:],
                                          wqk[j][:, dc * 128:(dc + 1) * 128],
                                          xT[dc][:, half * 512:(half + 1) * 512],
                                          start=st, stop=False), waits=waits)
                    mloc = m % 16
                    p.op("tensor", lambda e, mloc=mloc, half=half: e.matmul(
                        p_q[half][:], bqrow[0:1, mloc * 128:(mloc + 1) * 128],
                        ones_bf[:], start=False, stop=True),
                        waits=[bqv],
                        inc=s_mm if half == 1 else None)
                mmv = s_mm.n
                for half in range(2):
                    p.op("vector", lambda e, half=half: e.tensor_copy(
                        out=ebuf[:, half * 512:(half + 1) * 512],
                        in_=p_q[half][:]),
                        waits=[(s_mm, mmv), (s_st, s_st.n)], inc=s_v)
                dst = q_out if m < 16 else k_out
                mm2 = m if m < 16 else m - 16
                p.op("sync", lambda e, dst=dst, mm2=mm2: e.dma_start(
                    out=dst[mm2 * 128:(mm2 + 1) * 128, :], in_=ebuf[:]),
                    waits=[(s_v, s_v.n)], inc=s_st, inc_n=16)
                if mi + 2 < 16 or 24 <= mi + 2 < 32:
                    load_qk_panel(j, morder[mi + 2], [(s_mm, mmv)])

            for mi in range(16):
                qk_iter(mi)
            k_st = s_st.n

            wvcol = qs("wvcol", [128, 16 * 512], BF16)
            p.op("gpsimd", lambda e: e.dma_start(
                out=bqrow[:], in_=bqkv[None, 2 * D:3 * D]),
                waits=[(s_mm, s_mm.n)], inc=s_gld, inc_n=16)
            bqv_v = s_gld.n
            for vc in range(4):
                src = wqkv.rearrange("(dc p) m -> p dc m", p=128)[:, :, 2 * D + vc * 512: 2 * D + (vc + 1) * 512]
                p.op("gpsimd", lambda e, src=src: e.dma_start(
                    out=wvcol[:].rearrange("p (dc m) -> p dc m", dc=16), in_=src),
                    waits=[(s_mm, s_mm.n)], inc=s_gld, inc_n=16)
                wv = s_gld.n
                for ti in range(LT):
                    for dc in range(16):
                        st = (dc == 0)
                        waits = []
                        if st:
                            waits = [(s_gld, wv), (s_v, s_v.n)]
                        p.op("tensor", lambda e, dc=dc, st=st, ti=ti:
                             e.matmul(p_q[ti % 2][:],
                                      xT[dc][:, ti * 128:(ti + 1) * 128],
                                      wvcol[:, dc * 512:(dc + 1) * 512],
                                      start=st, stop=False), waits=waits)
                    p.op("tensor", lambda e, vc=vc, ti=ti: e.matmul(
                        p_q[ti % 2][:], ones_bf[0:1, 0:128],
                        bqrow[0:1, vc * 512:(vc + 1) * 512],
                        start=False, stop=True),
                        waits=[(s_gld, bqv_v)] if ti == 0 else [], inc=s_mm)
                    mmv = s_mm.n
                    p.op("vector", lambda e, ti=ti: e.tensor_copy(
                        out=ebuf[:, 0:512], in_=p_q[ti % 2][:]),
                        waits=[(s_mm, mmv), (s_st, s_st.n)], inc=s_v)
                    p.op("sync", lambda e, vc=vc, ti=ti: e.dma_start(
                        out=v_out[ti * 128:(ti + 1) * 128, vc * 512:(vc + 1) * 512],
                        in_=ebuf[:, 0:512]),
                        waits=[(s_v, s_v.n)], inc=s_st, inc_n=16)
            v_st = s_st.n

            wqk_all = qs("wqk_all", [128, 8 * D], BF16)
            p.op("gpsimd", lambda e: e.dma_start(
                out=bqrow[:], in_=bqkv[None, 0:D]),
                waits=[(s_mm, s_mm.n)], inc=s_gld, inc_n=16)
            bq_ld[1] = (s_gld, s_gld.n)
            for m8 in range(8):
                src = wqkv.rearrange("(dc p) m -> p dc m", p=128)[:, :, m8 * 128:(m8 + 1) * 128]
                p.op("gpsimd", lambda e, m8=m8, src=src: e.dma_start(
                    out=wqk_all[:, m8 * D:(m8 + 1) * D].rearrange(
                        "p (dc m) -> p dc m", dc=16),
                    in_=src),
                    inc=s_gld, inc_n=16)
                qk_pan[m8] = (s_gld, s_gld.n)

            kag_out = nc.dram_tensor("kag_out", [NCORES * D, LT * 128], BF16).ap()
            vag_out = nc.dram_tensor("vag_out", [T, D], BF16).ap()
            p.op("gpsimd", lambda e: e.collective_compute(
                "AllGather", ALU.bypass, replica_groups=[list(range(NCORES))],
                ins=[k_out.opt()], outs=[kag_out.opt()]),
                waits=[(s_st, k_st)], inc=s_col)
            p.op("gpsimd", lambda e: e.collective_compute(
                "AllGather", ALU.bypass, replica_groups=[list(range(NCORES))],
                ins=[v_out.opt()], outs=[vag_out.opt()]),
                waits=[(s_st, v_st)], inc=s_col)
            kv_col = s_col.n

            for mi in range(16, 24):
                qk_iter(mi, allbuf=True)
            for mi in range(24, 32):
                qk_iter(mi)

        _fence()
        p.wait("gpsimd", (s_col, kv_col))
        if debug:
            dbg1 = nc.dram_tensor("dbg_kin", [D, LT * 128], BF16, kind="ExternalOutput").ap()
            dbg2 = nc.dram_tensor("dbg_vin", [LT * 128, D], BF16, kind="ExternalOutput").ap()
            dbg3 = nc.dram_tensor("dbg_q", [D, LT * 128], BF16, kind="ExternalOutput").ap()
            dbg4 = nc.dram_tensor("dbg_kago", [NCORES * D, LT * 128], BF16, kind="ExternalOutput").ap()
            p.op("sync", lambda e: e.dma_start(out=dbg1, in_=k_out),
                 waits=[(s_st, s_st.n)], inc=s_st, inc_n=16)
            p.op("sync", lambda e: e.dma_start(out=dbg2, in_=v_out), inc=s_st, inc_n=16)
            p.op("sync", lambda e: e.dma_start(out=dbg3, in_=q_out), inc=s_st, inc_n=16)
            p.op("gpsimd", lambda e: e.dma_start(out=dbg4, in_=kag_out),
                 waits=[(s_col, kv_col)], inc=s_st, inc_n=16)

        q_in = q_out
        k_full = kag_out
        v_full = vag_out
        wo = nc.dram_tensor("wo", [D, D], F32, kind="ExternalInput").ap()
        bo = nc.dram_tensor("bo", [D], F32, kind="ExternalInput").ap()
        router = nc.dram_tensor("router", [D, 8], F32, kind="ExternalInput").ap()
        masksel = nc.dram_tensor("masksel", [128, NCORES * 128], BF16, kind="ExternalInput").ap()

        xbf_out = nc.dram_tensor("xbf_own_d", [LT * 128, D], BF16).ap()
        logit_out = nc.dram_tensor("logit_own_d", [LT * 128, 8], F32).ap()
        attn_dram = nc.dram_tensor("attn_dram", [LT * 128, D], BF16).ap()

        ALPHA = 1.0 / float(HD ** 0.5)
        with ExitStack() as l2es:
            sbt2 = lambda name, shape, dt: l2es.enter_context(nc.sbuf_tensor(name, shape, dt))
            maskb = sbt2("maskb", [128, NCORES * 128], BF16)
            id_f = sbt2("id_f2_s", [128, 128], F32)
            id_f_d2 = nc.inline_tensor(np.eye(128, dtype=np.float32), "id_f2").ap()
            p.op("sync", lambda e: e.dma_start(out=maskb[:], in_=masksel),
                 waits=[(s_mm, s_mm.n), (s_v, s_v.n), (s_s, s_s.n)], inc=s_ld, inc_n=16)
            p.op("sync", lambda e: e.dma_start(out=id_f[:], in_=id_f_d2), inc=s_ld, inc_n=16)
            ld_base = s_ld.n

            _fence()
            with ExitStack() as aes:
                asb = lambda name, shape, dt: aes.enter_context(nc.sbuf_tensor(name, shape, dt))
                aps = lambda name, shape, dt: aes.enter_context(nc.psum_tensor(name, shape, dt))
                kbuf = [asb(f"kbuf{j}", [128, T], BF16) for j in range(2)]
                vbuf = [asb(f"vbuf{j}", [128, NT * 132], BF16) for j in range(2)]
                qbuf = [asb(f"qbuf{j}", [128, LT * 128], BF16) for j in range(2)]
                ptb = [asb(f"ptb{j}", [128, 512], BF16) for j in range(2)]
                fstage = [asb(f"fstage{j}", [128, HD], BF16) for j in range(2)]
                mc = asb("mc", [128, 8], F32)
                p_sg = [aps(f"p_sg{j}", [128, 512], F32) for j in range(2)]
                p_av = aps("p_av", [128, 132], F32)

                ocol = np.zeros((128, NT, 4), np.float32)
                ocol[:, :, 0] = 1.0
                ocol_d = nc.inline_tensor(ocol.astype(BF), "att_ocol").ap()
                for jb in range(2):
                    vr = vbuf[jb][:].rearrange("p (b w) -> p b w", w=132)
                    p.op("sync", lambda e, vr=vr: e.dma_start(
                        out=vr[:, :, 128:132], in_=ocol_d), inc=s_ld, inc_n=16)
                vz_ld = s_ld.n
                vz = 0

                head_ld = {}
                head_gld = {}
                head_av = {}

                def load_head(h):
                    hb = h % 2
                    war = [(s_mm, head_av.get(h - 2, 0))]
                    for r in range(NCORES):
                        p.op("gpsimd", lambda e, h=h, r=r, hb=hb: e.dma_start(
                            out=kbuf[hb][:, r * 1024:(r + 1) * 1024],
                            in_=k_full[r * D + h * HD: r * D + (h + 1) * HD, :]),
                            waits=war if r == 0 else [], inc=s_gld, inc_n=16)
                    head_gld[h] = s_gld.n
                    p.op("sync", lambda e, h=h, hb=hb: e.dma_start(
                        out=vbuf[hb][:].rearrange("p (b w) -> p b w", w=132)[:, :, 0:128],
                        in_=v_full.rearrange("(b p) hd -> p b hd", p=128)[:, :, h * HD:(h + 1) * HD]),
                        waits=war + ([(s_ld, vz_ld)] if h < 2 else []),
                        inc=s_ld, inc_n=16)
                    p.op("sync", lambda e, h=h, hb=hb: e.dma_start(
                        out=qbuf[hb][:], in_=q_in[h * HD:(h + 1) * HD, :]),
                        inc=s_ld, inc_n=16)
                    head_ld[h] = s_ld.n

                load_head(0)
                load_head(1)
                psg_free = {0: 0, 1: 0}
                ptb_free = {0: 0, 1: 0}
                fst_free = {0: 0, 1: 0}
                fin_by_fj = {0: 0, 1: 0}
                fin_s = 0
                gctr = 0
                fctr = 0
                for h in range(NH):
                    hb = h % 2
                    for i in range(LT):
                        ngr = 2 * (i + 1)
                        exp_done = [0] * ngr
                        mask_done = [None] * ngr
                        jjs = [0] * ngr

                        def emit_av(g2, hb=hb, i=i, ngr=ngr, exp_done=exp_done,
                                    mask_done=mask_done, jjs=jjs):
                            jj2 = jjs[g2]
                            jloc2 = g2 // 2
                            rbase2 = (g2 % 2) * 4
                            for t in range(4):
                                r = rbase2 + t
                                b = 8 * r + jloc2
                                st = (g2 == 0 and t == 0)
                                sp = (g2 == ngr - 1 and t == 3)
                                w = []
                                if t == 0:
                                    w = ([(s_v, mask_done[g2])]
                                         if mask_done[g2] is not None
                                         else [(s_s, exp_done[g2])])
                                    if st:
                                        w += [(s_s, fin_s)]
                                p.op("tensor", lambda e, jj2=jj2, t=t, b=b, hb=hb, st=st, sp=sp:
                                     e.matmul(p_av[:],
                                              ptb[jj2][:, t * 128:(t + 1) * 128],
                                              vbuf[hb][:, b * 132:(b + 1) * 132],
                                              start=st, stop=sp),
                                     waits=w, inc=s_mm if t == 3 else None)
                            ptb_free[jj2] = s_mm.n

                        for g in range(ngr):
                            jj = gctr % 2
                            gctr += 1
                            jjs[g] = jj
                            jloc = g // 2
                            rbase = (g % 2) * 4
                            for t in range(4):
                                r = rbase + t
                                koff = r * 1024 + jloc * 128
                                w = []
                                if t == 0:
                                    w = [(s_s, psg_free[jj])]
                                    if g == 0:
                                        w += [(s_gld, head_gld[h]),
                                              (s_ld, head_ld[h])]
                                p.op("tensor", lambda e, jj=jj, t=t, koff=koff, hb=hb, i=i:
                                     e.matmul(p_sg[jj][:, t * 128:(t + 1) * 128],
                                              kbuf[hb][:, koff:koff + 128],
                                              qbuf[hb][:, i * 128:(i + 1) * 128],
                                              start=True, stop=True),
                                     waits=w, inc=s_mm if t == 3 else None)
                            st_mm = s_mm.n
                            p.op("scalar", lambda e, jj=jj: e.activation(
                                out=ptb[jj][:], in_=p_sg[jj][:], func=AF.Exp,
                                scale=ALPHA),
                                waits=[(s_mm, st_mm), (s_mm, ptb_free[jj])],
                                inc=s_s)
                            psg_free[jj] = s_s.n
                            exp_done[g] = s_s.n
                            if jloc == i:
                                p.op("vector", lambda e, jj=jj, rbase=rbase: e.tensor_mul(
                                    out=ptb[jj][:], in0=ptb[jj][:],
                                    in1=maskb[:, rbase * 128:(rbase + 4) * 128]),
                                    waits=[(s_s, exp_done[g])], inc=s_v)
                                mask_done[g] = s_v.n
                            if g >= 1:
                                emit_av(g - 1)
                        emit_av(ngr - 1)
                        av_mm = s_mm.n
                        head_av[h] = av_mm
                        fj = fctr % 2
                        fctr += 1
                        p.op("vector", lambda e, fj=fj: e.reciprocal(
                            out=mc[:, fj:fj + 1], in_=p_av[:, 128:129]),
                            waits=[(s_mm, av_mm), (s_s, fin_by_fj[fj])], inc=s_v)
                        rec_v = s_v.n
                        p.op("scalar", lambda e, fj=fj: e.activation(
                            out=fstage[fj][:], in_=p_av[:, 0:128],
                            func=AF.Identity, scale=mc[:, fj:fj + 1]),
                            waits=[(s_v, rec_v), (s_st, fst_free[fj])], inc=s_s)
                        fin_s = s_s.n
                        fin_by_fj[fj] = s_s.n
                        p.op("sync", lambda e, fj=fj, h=h, i=i: e.dma_start(
                            out=attn_dram[i * 128:(i + 1) * 128, h * HD:(h + 1) * HD],
                            in_=fstage[fj][:]),
                            waits=[(s_s, fin_s)], inc=s_st, inc_n=16)
                        fst_free[fj] = s_st.n
                    if h + 2 < NH:
                        load_head(h + 2)
                att_end = [(s_mm, s_mm.n), (s_v, s_v.n), (s_s, s_s.n),
                           (s_gld, s_gld.n), (s_st, s_st.n), (s_ld, s_ld.n)]

            _fence()
            with ExitStack() as pes:
                psb2 = lambda name, shape, dt: pes.enter_context(nc.sbuf_tensor(name, shape, dt))
                pps = lambda name, shape, dt: pes.enter_context(nc.psum_tensor(name, shape, dt))
                attnT = xT
                wocol = psb2("wocol", [128, 16 * 512], BF16)
                borow = psb2("borow", [1, D], BF16)
                abuf = [psb2(f"abuf{j}", [128, D], BF16) for j in range(2)]
                p_t2 = [pps(f"p_t2_d2{j}", [128, 512], BF16) for j in range(2)]
                p_pr = [pps(f"p_pr{j}", [128, 512], F32) for j in range(2)]

                p.op("gpsimd", lambda e: e.dma_start(out=borow[:], in_=bo[None, :]),
                     waits=att_end, inc=s_gld, inc_n=16)
                bo_ld = s_gld.n
                ab_mm = {}
                for i in range(LT):
                    p.op("sync", lambda e, i=i: e.dma_start(
                        out=abuf[i % 2][:],
                        in_=attn_dram[i * 128:(i + 1) * 128, :]),
                        waits=([(s_st, s_st.n)] if i == 0 else
                               [(s_mm, ab_mm[i - 2])] if i >= 2 else []),
                        inc=s_ld, inc_n=16)
                    lv = s_ld.n
                    grp_v = {}
                    for g4 in range(4):
                        jj = g4 % 2
                        for t in range(4):
                            cc = 4 * g4 + t
                            w = []
                            if t == 0:
                                if g4 <= 1:
                                    w = [(s_ld, lv), (s_v, s_v.n)]
                                else:
                                    w = [(s_v, grp_v[g4 - 2])]
                            p.op("tensor", lambda e, jj=jj, t=t, cc=cc, i=i:
                                 e.transpose(
                                     p_t2[jj][:, t * 128:(t + 1) * 128],
                                     abuf[i % 2][:, cc * 128:(cc + 1) * 128],
                                     id_bf[:]),
                                 waits=w, inc=s_mm if t == 3 else None)
                        mmv2 = s_mm.n
                        for t in range(4):
                            cc = 4 * g4 + t
                            p.op("vector", lambda e, jj=jj, t=t, cc=cc, i=i:
                                 e.tensor_copy(
                                     out=attnT[cc][:, i * 128:(i + 1) * 128],
                                     in_=p_t2[jj][:, t * 128:(t + 1) * 128]),
                                 waits=[(s_mm, mmv2)] if t == 0 else [],
                                 inc=s_v if t == 3 else None)
                        grp_v[g4] = s_v.n
                    ab_mm[i] = s_mm.n
                tr_done = [(s_mm, s_mm.n), (s_v, s_v.n)]
                for dc in range(4):
                    p.op("gpsimd", lambda e, dc=dc: e.dma_start(
                        out=wocol[:].rearrange("p (k n) -> p k n", k=16),
                        in_=wo.rearrange("(k p) n -> p k n", p=128)[:, :, dc * 512:(dc + 1) * 512]),
                        waits=(att_end + [(s_mm, s_mm.n)]) if dc == 0 else [(s_mm, s_mm.n)],
                        inc=s_gld, inc_n=16)
                    wv = s_gld.n
                    for ti in range(LT):
                        for cc in range(16):
                            st = (cc == 0)
                            waits = []
                            if st:
                                waits = [(s_gld, wv), (s_v, s_v.n)] + \
                                    (tr_done if ti == 0 else [])
                            p.op("tensor", lambda e, cc=cc, ti=ti, st=st:
                                 e.matmul(p_pr[ti % 2][:],
                                          attnT[cc][:, ti * 128:(ti + 1) * 128],
                                          wocol[:, cc * 512:(cc + 1) * 512],
                                          start=st, stop=False), waits=waits)
                        p.op("tensor", lambda e, dc=dc, ti=ti: e.matmul(
                            p_pr[ti % 2][:], ones_bf[0:1, 0:128],
                            borow[0:1, dc * 512:(dc + 1) * 512],
                            start=False, stop=True),
                            waits=[(s_gld, bo_ld)] if (dc == 0 and ti == 0) else [],
                            inc=s_mm)
                        mmv = s_mm.n
                        p.op("vector", lambda e, dc=dc, ti=ti: e.tensor_add(
                            out=stream[ti][:, dc * 512:(dc + 1) * 512],
                            in0=p_pr[ti % 2][:],
                            in1=stream[ti][:, dc * 512:(dc + 1) * 512]),
                            waits=[(s_mm, mmv)], inc=s_v)

                routs = psb2("routs", [128, 16 * 8], F32)
                dbg_routs_holder.append(routs)
                xt32 = psb2("xt32", [128, 128], F32)
                lgb = psb2("lgb", [128, 8], F32)
                p_t32 = pps("p_t32", [128, 128], F32)
                p_r = pps("p_r", [128, 8], F32)
                p.op("sync", lambda e: e.dma_start(
                    out=routs[:].rearrange("p (dc e) -> p dc e", dc=16),
                    in_=router.rearrange("(dc p) e -> p dc e", p=128)),
                    inc=s_ld, inc_n=16)
                rt_ld = s_ld.n
                opj_mm = s_mm.n
                for i in range(LT):
                    emit_layernorm(p, stream[i][:], lnscr[:], sqdump[:], col8[:],
                                   s_v, s_s, out_bf=xbf_scr[:], bf_war=(s_st,))
                    sv_ln = s_v.n
                    ss_ln = s_s.n
                    p.op("sync", lambda e, i=i: e.dma_start(
                        out=xbf_out[i * 128:(i + 1) * 128, :], in_=xbf_scr[:]),
                        waits=[(s_s, ss_ln)], inc=s_st, inc_n=16)
                    for cc in range(16):
                        p.op("tensor", lambda e, cc=cc, i=i: e.transpose(
                            p_t32[:], stream[i][:, cc * 128:(cc + 1) * 128], id_f[:]),
                            waits=[(s_v, s_v.n)], inc=s_mm)
                        p.op("vector", lambda e: e.tensor_copy(out=xt32[:], in_=p_t32[:]),
                             waits=[(s_mm, s_mm.n)], inc=s_v)
                        p.op("vector", lambda e, cc=cc, i=i: e.tensor_copy(
                            out=xT[cc][:, i * 128:(i + 1) * 128], in_=p_t32[:]),
                            waits=([(s_mm, opj_mm)] if (i == 0 and cc == 0) else []),
                            inc=s_v)
                        vv = s_v.n
                        p.op("tensor", lambda e, cc=cc: e.matmul(
                            p_r[:], xt32[:], routs[:, cc * 8:(cc + 1) * 8],
                            start=(cc == 0), stop=(cc == 15)),
                            waits=[(s_v, vv)] + ([(s_ld, rt_ld)] if cc == 0 else []),
                            inc=s_mm if cc == 15 else None)
                    p.op("vector", lambda e, i=i: e.tensor_copy(
                        out=lgb[:], in_=p_r[:]),
                        waits=[(s_mm, s_mm.n), (s_st, s_st.n)], inc=s_v)
                    p.op("sync", lambda e, i=i: e.dma_start(
                        out=logit_out[i * 128:(i + 1) * 128, :], in_=lgb[:]),
                        waits=[(s_v, s_v.n)], inc=s_st, inc_n=16)



        _fence()
        xag_out = nc.dram_tensor("xag_out", [T, D], BF16).ap()
        lgag_out = nc.dram_tensor("lgag_out", [T, 8], F32).ap()
        p.op("gpsimd", lambda e: e.collective_compute(
            "AllGather", ALU.bypass, replica_groups=[list(range(NCORES))],
            ins=[xbf_out.opt()], outs=[xag_out.opt()]),
            waits=[(s_st, s_st.n)], inc=s_col)
        p.op("gpsimd", lambda e: e.collective_compute(
            "AllGather", ALU.bypass, replica_groups=[list(range(NCORES))],
            ins=[logit_out.opt()], outs=[lgag_out.opt()]),
            inc=s_col)
        xlg_col = s_col.n
        p.wait("gpsimd", (s_col, xlg_col))
        if debug:
            dbg5 = nc.dram_tensor("dbg_xbf", [LT * 128, D], BF16, kind="ExternalOutput").ap()
            dbg6 = nc.dram_tensor("dbg_lg", [LT * 128, 8], F32, kind="ExternalOutput").ap()
            p.op("sync", lambda e: e.dma_start(out=dbg5, in_=xbf_out),
                 waits=[(s_st, s_st.n)], inc=s_st, inc_n=16)
            p.op("sync", lambda e: e.dma_start(out=dbg6, in_=logit_out), inc=s_st, inc_n=16)
            dbg9 = nc.dram_tensor("dbg_routs", [128, 16 * 8], F32, kind="ExternalOutput").ap()
            p.op("sync", lambda e: e.dma_start(out=dbg9, in_=dbg_routs_holder[0][:]), inc=s_st, inc_n=16)

        xbf_own = xbf_out
        x_full = xag_out
        logits_full = lgag_out
        my_e = nc.dram_tensor("my_e", [128, 8], F32, kind="ExternalInput").ap()
        w_ug_e = nc.dram_tensor("w_ug_e", [D, 2 * I_EXP], F32, kind="ExternalInput").ap()
        w_dn_e = nc.dram_tensor("w_dn_e", [I_EXP, D], F32, kind="ExternalInput").ap()
        w_sug = nc.dram_tensor("w_sug", [D, 2 * J_SH], F32, kind="ExternalInput").ap()
        w_sd = nc.dram_tensor("w_sd", [J_SH, D], F32, kind="ExternalInput").ap()

        y_part = nc.dram_tensor("y_part_d", [T, D], BF16).ap()

        buf_dram = nc.dram_tensor("buf_dram", [CAP, 2056], BF16).ap()
        h2_dram = nc.dram_tensor("h2_dram", [J_SH, LT * 128], BF16).ap()
        tri128_d = nc.inline_tensor(
            np.tril(np.ones((128, 128), np.float32), -1).T.copy(), "tri128").ap()
        tri64_d = nc.inline_tensor(
            np.tril(np.ones((64, 64), np.float32), -1).T.copy(), "tri64").ap()
        id64_d = nc.inline_tensor(np.eye(64, dtype=np.float32), "id64").ap()
        pi_vals = np.zeros((128, NT), np.float64)
        for g in range(NT):
            for pp2 in range(128):
                pi_vals[pp2, g] = pi_row_block(g) + pp2
        pihi = pi_vals.astype(BF)
        pilo = (pi_vals - pihi.astype(np.float64)).astype(BF)
        pihi_d = nc.inline_tensor(pihi, "pihi").ap()
        pilo_d = nc.inline_tensor(pilo, "pilo").ap()

        xT2 = xT
        onescol = sbt("onescol", [128, 1], F32)
        regs = {}
        p.q["gpsimd"].append(lambda e: regs.update(
            cap=e.to_reg(CAP - 1), tmax=e.to_reg(T - 1)))
        p.op("vector", lambda e: e.memset(onescol[:], 1.0),
             waits=[(s_s, s_s.n), (s_mm, s_mm.n)], inc=s_v)
        ld_base = s_ld.n

        xt_done = [(s_mm, s_mm.n), (s_v, s_v.n)]

        _fence()
        with ExitStack() as res_:
            rsb = lambda name, shape, dt: res_.enter_context(nc.sbuf_tensor(name, shape, dt))
            rps = lambda name, shape, dt: res_.enter_context(nc.psum_tensor(name, shape, dt))
            lgt = rsb("lgt", [128, 8], F32)
            mye = rsb("mye_s", [128, 8], F32)
            tri128 = rsb("tri128_s", [128, 128], F32)
            tri64 = rsb("tri64_s", [64, 64], F32)
            id64 = rsb("id64_s", [64, 64], F32)
            pihi_s = rsb("pihi_sb", [128, NT], BF16)
            pilo_s = rsb("pilo_sb", [128, NT], BF16)
            selb = rsb("selb", [128, NT], F32)
            mwb = rsb("mwb", [128, NT], F32)
            sc8 = rsb("sc8", [128, 16], F32)
            tmp8 = rsb("tmp8", [128, 8], F32)
            cntc = rsb("cntc", [64, 1], F32)
            basec = rsb("basec", [64, 1], F32)
            baserow = rsb("baserow", [1, 64], F32)
            idxi = rsb("idxi", [128, 1], I32)
            cmpt = rsb("cmpt", [128, 2056], BF16)
            fillt = rsb("fillt", [128, 2056], BF16)
            zt = rsb("zt", [128, D], BF16)
            p_r1 = rps("p_r1", [128, 128], F32)

            first = [True]
            for nm, dst, srcd in (("mye", mye, my_e), ("t128", tri128, tri128_d),
                                  ("t64", tri64, tri64_d), ("i64", id64, id64_d),
                                  ("ph", pihi_s, pihi_d), ("pl", pilo_s, pilo_d)):
                p.op("sync", lambda e, dst=dst, srcd=srcd: e.dma_start(
                    out=dst[:], in_=srcd),
                    waits=xt_done if first[0] else [], inc=s_ld, inc_n=16)
                first[0] = False
            ld2 = s_ld.n
            p.op("vector", lambda e: e.memset(fillt[:], 30000.0), inc=s_v)
            p.op("vector", lambda e: e.memset(zt[:], 0.0), inc=s_v)
            zv = s_v.n
            for sc_i in range(CAP // 128):
                p.op("sync", lambda e, sc_i=sc_i: e.dma_start(
                    out=buf_dram[sc_i * 128:(sc_i + 1) * 128, :], in_=fillt[:]),
                    waits=[(s_v, zv)] if sc_i == 0 else [], inc=s_fz, inc_n=16)
            for g in range(NT):
                p.op("sync", lambda e, g=g: e.dma_start(
                    out=y_part[g * 128:(g + 1) * 128, :], in_=zt[:]),
                    waits=[(s_v, zv)] if g == 0 else [], inc=s_fz, inc_n=16)
            fz_all = s_fz.n

            for g in range(NT):
                rb = pi_row_block(g)
                p.op("sync", lambda e, rb=rb: e.dma_start(
                    out=lgt[:], in_=logits_full[rb:rb + 128, :]),
                    waits=[(s_v, s_v.n)], inc=s_ld, inc_n=16)
                lv = s_ld.n
                p.op("vector", lambda e: e.reduce_max(out=sc8[:, 0:1], in_=lgt[:],
                                                      axis=AX),
                     waits=[(s_ld, lv)], inc=s_v)
                p.op("vector", lambda e: e.tensor_scalar(
                    out=tmp8[:], in0=lgt[:], scalar1=sc8[:, 0:1], scalar2=None,
                    op0=ALU.is_equal),
                    waits=[(s_v, s_v.n), (s_ld, ld2)], inc=s_v)
                p.op("vector", lambda e: e.scalar_tensor_tensor(
                    out=tmp8[:], in0=tmp8[:], scalar=-1e30, in1=lgt[:],
                    op0=ALU.mult, op1=ALU.add),
                    waits=[(s_v, s_v.n)], inc=s_v)
                p.op("vector", lambda e: e.reduce_max(out=sc8[:, 1:2], in_=tmp8[:],
                                                      axis=AX),
                     waits=[(s_v, s_v.n)], inc=s_v)
                p.op("vector", lambda e: e.tensor_sub(out=sc8[:, 2:3],
                                                      in0=sc8[:, 0:1],
                                                      in1=sc8[:, 1:2]),
                     waits=[(s_v, s_v.n)], inc=s_v)
                p.op("scalar", lambda e: e.activation(out=sc8[:, 3:4],
                                                      in_=sc8[:, 2:3],
                                                      func=AF.Sigmoid),
                     waits=[(s_v, s_v.n)], inc=s_s)
                p.op("scalar", lambda e: e.activation(out=sc8[:, 4:5],
                                                      in_=sc8[:, 2:3],
                                                      func=AF.Sigmoid, scale=-1.0),
                     waits=[(s_s, s_s.n)], inc=s_s)
                p.op("vector", lambda e: e.tensor_mul(out=tmp8[:], in0=lgt[:],
                                                      in1=mye[:]),
                     waits=[(s_v, s_v.n), (s_ld, ld2)], inc=s_v)
                p.op("vector", lambda e: e.reduce_sum(out=sc8[:, 5:6], in_=tmp8[:],
                                                      axis=AX),
                     waits=[(s_v, s_v.n)], inc=s_v)
                p.op("vector", lambda e: e.tensor_scalar(
                    out=sc8[:, 6:7], in0=sc8[:, 5:6], scalar1=sc8[:, 0:1],
                    scalar2=None, op0=ALU.is_equal),
                    waits=[(s_v, s_v.n)], inc=s_v)
                p.op("vector", lambda e: e.tensor_scalar(
                    out=sc8[:, 7:8], in0=sc8[:, 5:6], scalar1=sc8[:, 1:2],
                    scalar2=None, op0=ALU.is_equal),
                    waits=[(s_v, s_v.n)], inc=s_v)
                p.op("vector", lambda e: e.tensor_scalar_mul(
                    out=tmp8[:, 0:1], in0=sc8[:, 6:7], scalar1=sc8[:, 3:4]),
                    waits=[(s_v, s_v.n), (s_s, s_s.n)], inc=s_v)
                p.op("vector", lambda e: e.tensor_scalar_mul(
                    out=tmp8[:, 1:2], in0=sc8[:, 7:8], scalar1=sc8[:, 4:5]),
                    waits=[(s_v, s_v.n)], inc=s_v)
                p.op("vector", lambda e, g=g: e.tensor_add(
                    out=mwb[:, g:g + 1], in0=tmp8[:, 0:1], in1=tmp8[:, 1:2]),
                    waits=[(s_v, s_v.n)], inc=s_v)
                p.op("vector", lambda e, g=g: e.tensor_add(
                    out=selb[:, g:g + 1], in0=sc8[:, 6:7], in1=sc8[:, 7:8]),
                    waits=[(s_v, s_v.n)], inc=s_v)

            p.op("tensor", lambda e: e.matmul(
                p_r1[0:64, 0:1], selb[:], onescol[:], start=True, stop=True),
                waits=[(s_v, s_v.n)], inc=s_mm)
            p.op("vector", lambda e: e.tensor_copy(out=cntc[:], in_=p_r1[0:64, 0:1]),
                 waits=[(s_mm, s_mm.n)], inc=s_v)
            p.op("tensor", lambda e: e.matmul(
                p_r1[0:64, 0:1], tri64[:], cntc[:], start=True, stop=True),
                waits=[(s_v, s_v.n)], inc=s_mm)
            p.op("vector", lambda e: e.tensor_copy(out=basec[:], in_=p_r1[0:64, 0:1]),
                 waits=[(s_mm, s_mm.n)], inc=s_v)
            p.op("tensor", lambda e: e.matmul(
                p_r1[0:1, 0:64], basec[:], id64[:], start=True, stop=True),
                waits=[(s_v, s_v.n)], inc=s_mm)
            p.op("vector", lambda e: e.tensor_copy(out=baserow[:], in_=p_r1[0:1, 0:64]),
                 waits=[(s_mm, s_mm.n)], inc=s_v)

            for g in range(NT):
                rb = pi_row_block(g)
                p.op("tensor", lambda e, g=g: e.matmul(
                    p_r1[:, 0:1], tri128[:], selb[:, g:g + 1],
                    start=True, stop=False),
                    waits=[(s_v, s_v.n), (s_ld, ld2)])
                p.op("tensor", lambda e, g=g: e.matmul(
                    p_r1[:, 0:1], ones_f[0:1, 0:128], baserow[0:1, g:g + 1],
                    start=False, stop=True), inc=s_mm)
                mmv = s_mm.n
                p.op("vector", lambda e: e.tensor_copy(out=sc8[:, 10:11],
                                                       in_=p_r1[:, 0:1]),
                     waits=[(s_mm, mmv)], inc=s_v)
                p.op("vector", lambda e: e.tensor_scalar(
                    out=sc8[:, 11:12], in0=sc8[:, 10:11], scalar1=float(CAP),
                    scalar2=None, op0=ALU.is_lt),
                    waits=[(s_v, s_v.n)], inc=s_v)
                p.op("vector", lambda e, g=g: e.tensor_mul(
                    out=sc8[:, 12:13], in0=selb[:, g:g + 1], in1=sc8[:, 11:12]),
                    waits=[(s_v, s_v.n)], inc=s_v)
                p.op("vector", lambda e: e.tensor_mul(
                    out=sc8[:, 13:14], in0=sc8[:, 10:11], in1=sc8[:, 12:13]),
                    waits=[(s_v, s_v.n)], inc=s_v)
                p.op("vector", lambda e: e.tensor_scalar(
                    out=sc8[:, 14:15], in0=sc8[:, 12:13], scalar1=-1e9,
                    scalar2=1e9, op0=ALU.mult, op1=ALU.add),
                    waits=[(s_v, s_v.n)], inc=s_v)
                p.op("vector", lambda e: e.tensor_add(
                    out=sc8[:, 13:14], in0=sc8[:, 13:14], in1=sc8[:, 14:15]),
                    waits=[(s_v, s_v.n)], inc=s_v)
                p.op("vector", lambda e: e.tensor_copy(out=idxi[:],
                                                       in_=sc8[:, 13:14]),
                     waits=[(s_v, s_v.n), (s_sc, s_sc.n)], inc=s_v)
                p.op("vector", lambda e, g=g: e.tensor_mul(
                    out=sc8[:, 15:16], in0=mwb[:, g:g + 1], in1=sc8[:, 11:12]),
                    waits=[(s_v, s_v.n)], inc=s_v)
                p.op("scalar", lambda e: e.copy(out=cmpt[:, 2048:2049],
                                                in_=sc8[:, 15:16]),
                     waits=[(s_v, s_v.n), (s_sc, s_sc.n)], inc=s_s)
                p.op("vector", lambda e: e.tensor_sub(
                    out=cmpt[:, 2049:2050], in0=sc8[:, 15:16],
                    in1=cmpt[:, 2048:2049]),
                    waits=[(s_s, s_s.n)], inc=s_v)
                p.op("vector", lambda e, g=g: e.tensor_copy(
                    out=cmpt[:, 2050:2051], in_=pihi_s[:, g:g + 1]),
                    waits=[(s_v, s_v.n)], inc=s_v)
                p.op("vector", lambda e, g=g: e.tensor_copy(
                    out=cmpt[:, 2051:2052], in_=pilo_s[:, g:g + 1]),
                    waits=[(s_v, s_v.n)], inc=s_v)
                p.op("sync", lambda e, rb=rb: e.dma_start(
                    out=cmpt[:, 0:2048], in_=x_full[rb:rb + 128, :]),
                    waits=[(s_sc, s_sc.n)], inc=s_ld, inc_n=16)
                xv = s_ld.n
                vv = s_v.n
                p.op("gpsimd", lambda e: e.indirect_dma_start(
                    out=buf_dram, out_offset=IndirectOffsetOnAxis(idxi[:, 0:1], 0),
                    in_=cmpt[:], in_offset=None,
                    bounds_check=regs["cap"], oob_is_err=False),
                    waits=[(s_ld, xv), (s_v, vv), (s_s, s_s.n), (s_fz, fz_all)],
                    inc=s_sc, inc_n=16)
            scat_all = s_sc.n
            rout_end = [(s_mm, s_mm.n), (s_v, s_v.n), (s_s, s_s.n),
                        (s_g, s_g.n), (s_sc, s_sc.n), (s_ld, s_ld.n)]

        _fence()
        with ExitStack() as ees:
            esb = lambda name, shape, dt: ees.enter_context(nc.sbuf_tensor(name, shape, dt))
            eps_ = lambda name, shape, dt: ees.enter_context(nc.psum_tensor(name, shape, dt))
            wdn_s = esb("wdn_s", [128, 11 * D], BF16)
            brow = esb("brow", [128, 2056], BF16)
            bufT = [esb(f"bufT{cc}", [128, SCW], BF16) for cc in range(16)]
            hexp = [esb(f"hexp{k}", [128, SCW], BF16) for k in range(11)]
            wpan = [esb(f"wpan{j}", [128, D], BF16) for j in range(2)]
            wsb = esb("wsb", [128, 4], F32)
            pib = esb("pib", [128, 4], I32)
            pcol = esb("pcol", [128, 1], F32)
            gsc = esb("gsc", [128, 512], BF16)
            orow = esb("orow", [128, D], BF16)
            p_t3 = eps_("p_t3", [128, 128], BF16)
            p_ge = eps_("p_ge", [128, 512], F32)
            p_ue = eps_("p_ue", [128, 512], F32)
            p_de = eps_("p_de", [128, 512], F32)

            p.op("gpsimd", lambda e: e.dma_start(
                out=wdn_s[:].rearrange("p (k n) -> p k n", k=11),
                in_=w_dn_e.rearrange("(k p) n -> p k n", p=128)),
                waits=rout_end, inc=s_gld, inc_n=16)
            wdn_ld = s_gld.n

            for sch in range(NSC):
                for sub in range(4):
                    row0 = sch * SCW + sub * 128
                    p.op("gpsimd", lambda e, row0=row0: e.dma_start(
                        out=brow[:], in_=buf_dram[row0:row0 + 128, :]),
                        waits=[(s_sc, s_sc.n), (s_mm, s_mm.n), (s_v, s_v.n),
                               (s_g, s_g.n)],
                        inc=s_gld, inc_n=16)
                    bv2 = s_gld.n
                    p.op("vector", lambda e, sub=sub: e.tensor_add(
                        out=wsb[:, sub:sub + 1], in0=brow[:, 2048:2049],
                        in1=brow[:, 2049:2050]),
                        waits=[(s_gld, bv2)], inc=s_v)
                    p.op("vector", lambda e: e.tensor_add(
                        out=pcol[:], in0=brow[:, 2050:2051],
                        in1=brow[:, 2051:2052]),
                        waits=[(s_v, s_v.n)], inc=s_v)
                    p.op("vector", lambda e, sub=sub: e.tensor_copy(
                        out=pib[:, sub:sub + 1], in_=pcol[:]),
                        waits=[(s_v, s_v.n)], inc=s_v)
                    for cc in range(16):
                        p.op("tensor", lambda e, cc=cc: e.transpose(
                            p_t3[:], brow[:, cc * 128:(cc + 1) * 128], id_bf[:]),
                            waits=[(s_gld, bv2), (s_v, s_v.n)], inc=s_mm)
                        p.op("vector", lambda e, cc=cc, sub=sub: e.tensor_copy(
                            out=bufT[cc][:, sub * 128:(sub + 1) * 128], in_=p_t3[:]),
                            waits=[(s_mm, s_mm.n)], inc=s_v)
                bt_done_v = s_v.n
                wsb_v = s_v.n

                for m in range(11):
                    j = m % 2
                    src = w_ug_e.rearrange("(dc p) m -> p dc m", p=128)[:, :, m * 128:(m + 1) * 128]
                    p.op("gpsimd", lambda e, src=src, j=j: e.dma_start(
                        out=wpan[j][:].rearrange("p (dc m) -> p dc m", dc=16),
                        in_=src),
                        waits=[(s_mm, s_mm.n), (s_v, bt_done_v)],
                        inc=s_gld, inc_n=16)
                    gv2 = s_gld.n
                    for dc in range(16):
                        st = (dc == 0)
                        sp = (dc == 15)
                        waits = [(s_gld, gv2), (s_s, s_s.n), (s_v, s_v.n)] if st else []
                        p.op("tensor", lambda e, j=j, dc=dc, st=st, sp=sp:
                             e.matmul(p_ge[:], wpan[j][:, dc * 128:(dc + 1) * 128],
                                      bufT[dc][:], start=st, stop=sp),
                             waits=waits)
                    src = w_ug_e.rearrange("(dc p) m -> p dc m", p=128)[:, :, I_EXP + m * 128: I_EXP + (m + 1) * 128]
                    p.op("gpsimd", lambda e, src=src, j=j: e.dma_start(
                        out=wpan[(j + 1) % 2][:].rearrange("p (dc m) -> p dc m", dc=16),
                        in_=src),
                        waits=[(s_mm, s_mm.n)], inc=s_gld, inc_n=16)
                    uv2 = s_gld.n
                    for dc in range(16):
                        st = (dc == 0)
                        sp = (dc == 15)
                        waits = [(s_gld, uv2)] if st else []
                        p.op("tensor", lambda e, j=j, dc=dc, st=st, sp=sp:
                             e.matmul(p_ue[:], wpan[(j + 1) % 2][:, dc * 128:(dc + 1) * 128],
                                      bufT[dc][:], start=st, stop=sp),
                             waits=waits, inc=s_mm if sp else None)
                    mmv = s_mm.n
                    p.op("scalar", lambda e: e.activation(
                        out=gsc[:], in_=p_ge[:], func=AF.Silu),
                        waits=[(s_mm, mmv), (s_v, s_v.n)], inc=s_s)
                    p.op("vector", lambda e, m=m: e.scalar_tensor_tensor(
                        out=hexp[m][:], in0=p_ue[:], scalar=1.0, in1=gsc[:],
                        op0=ALU.mult, op1=ALU.mult),
                        waits=[(s_s, s_s.n)], inc=s_v)
                gu_v = s_v.n

                for sub in range(4):
                    for dc in range(4):
                        for k in range(11):
                            st = (k == 0)
                            sp = (k == 10)
                            waits = []
                            if st:
                                waits = [(s_v, gu_v), (s_gld, wdn_ld),
                                         (s_s, s_s.n)]
                            p.op("tensor", lambda e, k=k, st=st, sp=sp, sub=sub, dc=dc:
                                 e.matmul(p_de[:],
                                          hexp[k][:, sub * 128:(sub + 1) * 128],
                                          wdn_s[:, (k * 4 + dc) * 512:(k * 4 + dc + 1) * 512],
                                          start=st, stop=sp),
                                 waits=waits, inc=s_mm if sp else None)
                        mmv = s_mm.n
                        p.op("scalar", lambda e, dc=dc, sub=sub: e.activation(
                            out=orow[:, dc * 512:(dc + 1) * 512], in_=p_de[:],
                            func=AF.Identity, scale=wsb[:, sub:sub + 1]),
                            waits=[(s_mm, mmv), (s_v, wsb_v), (s_sc, s_sc.n)],
                            inc=s_s)
                    sv2 = s_s.n
                    p.op("gpsimd", lambda e, sub=sub: e.indirect_dma_start(
                        out=y_part, out_offset=IndirectOffsetOnAxis(pib[:, sub:sub + 1], 0),
                        in_=orow[:], in_offset=None,
                        bounds_check=regs["tmax"], oob_is_err=False),
                        waits=[(s_s, sv2), (s_fz, fz_all), (s_v, s_v.n)],
                        inc=s_sc, inc_n=16)
            exp_end = [(s_mm, s_mm.n), (s_v, s_v.n), (s_s, s_s.n),
                       (s_g, s_g.n), (s_sc, s_sc.n), (s_gld, s_gld.n)]

        _fence()
        with ExitStack() as ses:
            ssb = lambda name, shape, dt: ses.enter_context(nc.sbuf_tensor(name, shape, dt))
            sps = lambda name, shape, dt: ses.enter_context(nc.psum_tensor(name, shape, dt))
            wpan2 = [ssb(f"wpan2{j}", [128, D], BF16) for j in range(2)]
            fuse2 = ssb("fuse2", [128, LT * 128], BF16)
            gsc2 = ssb("gsc2", [128, 512], BF16)
            p_g2 = [sps(f"p_g2{j}", [128, 512], F32) for j in range(2)]
            p_u2 = [sps(f"p_u2{j}", [128, 512], F32) for j in range(2)]

            for m in range(44):
                j = m % 2
                src = w_sug.rearrange("(dc p) m -> p dc m", p=128)[:, :, m * 128:(m + 1) * 128]
                p.op("gpsimd", lambda e, src=src, j=j: e.dma_start(
                    out=wpan2[j][:].rearrange("p (dc m) -> p dc m", dc=16), in_=src),
                    waits=(exp_end if m == 0 else [(s_mm, s_mm.n)]),
                    inc=s_gld, inc_n=16)
                gv3 = s_gld.n
                for half in range(2):
                    for dc in range(16):
                        st = (dc == 0)
                        sp = (dc == 15)
                        waits = [(s_gld, gv3), (s_s, s_s.n), (s_v, s_v.n)] if (st and half == 0) else []
                        p.op("tensor", lambda e, j=j, dc=dc, st=st, sp=sp, half=half:
                             e.matmul(p_g2[half][:],
                                      wpan2[j][:, dc * 128:(dc + 1) * 128],
                                      xT2[dc][:, half * 512:(half + 1) * 512],
                                      start=st, stop=sp), waits=waits)
                src = w_sug.rearrange("(dc p) m -> p dc m", p=128)[:, :, J_SH + m * 128: J_SH + (m + 1) * 128]
                p.op("gpsimd", lambda e, src=src, j=j: e.dma_start(
                    out=wpan2[(j + 1) % 2][:].rearrange("p (dc m) -> p dc m", dc=16),
                    in_=src),
                    waits=[(s_mm, s_mm.n)], inc=s_gld, inc_n=16)
                uv3 = s_gld.n
                for half in range(2):
                    for dc in range(16):
                        st = (dc == 0)
                        sp = (dc == 15)
                        waits = [(s_gld, uv3)] if (st and half == 0) else []
                        p.op("tensor", lambda e, j=j, dc=dc, st=st, sp=sp, half=half:
                             e.matmul(p_u2[half][:],
                                      wpan2[(j + 1) % 2][:, dc * 128:(dc + 1) * 128],
                                      xT2[dc][:, half * 512:(half + 1) * 512],
                                      start=st, stop=sp),
                             waits=waits, inc=s_mm if (sp and half == 1) else None)
                mmv = s_mm.n
                for half in range(2):
                    p.op("scalar", lambda e, half=half: e.activation(
                        out=gsc2[:], in_=p_g2[half][:], func=AF.Silu),
                        waits=[(s_mm, mmv), (s_v, s_v.n)], inc=s_s)
                    p.op("vector", lambda e, half=half: e.scalar_tensor_tensor(
                        out=fuse2[:, half * 512:(half + 1) * 512],
                        in0=p_u2[half][:], scalar=1.0, in1=gsc2[:],
                        op0=ALU.mult, op1=ALU.mult),
                        waits=[(s_s, s_s.n), (s_fz, s_fz.n)], inc=s_v)
                p.op("sync", lambda e, m=m: e.dma_start(
                    out=h2_dram[m * 128:(m + 1) * 128, :], in_=fuse2[:]),
                    waits=[(s_v, s_v.n)], inc=s_fz, inc_n=16)
            h2_all = s_fz.n
            wsd_s = ssb("wsd_s", [128, 44 * 512], BF16)
            hpan2 = ssb("hpan2", [128, 44 * 128], BF16)
            p_d2 = [sps(f"p_d2{j}", [128, 512], F32) for j in range(2)]
            for dc in range(4):
                src = w_sd.rearrange("(k p) n -> p k n", p=128)[:, :, dc * 512:(dc + 1) * 512]
                p.op("gpsimd", lambda e, src=src: e.dma_start(
                    out=wsd_s[:].rearrange("p (k n) -> p k n", k=44), in_=src),
                    waits=[(s_mm, s_mm.n), (s_s, s_s.n), (s_v, s_v.n)],
                    inc=s_gld, inc_n=16)
                wv3 = s_gld.n
                for ti in range(LT):
                    hsrc = h2_dram.rearrange("(k p) t -> p k t", p=128)[:, :, ti * 128:(ti + 1) * 128]
                    p.op("gpsimd", lambda e, hsrc=hsrc: e.dma_start(
                        out=hpan2[:].rearrange("p (k t) -> p k t", k=44), in_=hsrc),
                        waits=[(s_fz, h2_all), (s_mm, s_mm.n)],
                        inc=s_gld, inc_n=16)
                    hv3 = s_gld.n
                    for k in range(44):
                        st = (k == 0)
                        sp = (k == 43)
                        waits = [(s_gld, hv3), (s_v, s_v.n)] if st else []
                        p.op("tensor", lambda e, k=k, st=st, sp=sp, ti=ti:
                             e.matmul(p_d2[ti % 2][:],
                                      hpan2[:, k * 128:(k + 1) * 128],
                                      wsd_s[:, k * 512:(k + 1) * 512],
                                      start=st, stop=sp),
                             waits=waits, inc=s_mm if sp else None)
                    mmv = s_mm.n
                    p.op("vector", lambda e, dc=dc, ti=ti: e.tensor_add(
                        out=stream[ti][:, dc * 512:(dc + 1) * 512],
                        in0=p_d2[ti % 2][:],
                        in1=stream[ti][:, dc * 512:(dc + 1) * 512]),
                        waits=[(s_mm, mmv)], inc=s_v)



        _fence()
        y_own = nc.dram_tensor("y_own_d", [LT * 128, D], BF16).ap()
        out_own = nc.dram_tensor("out_own", [LT * 128, D], F32, kind="ExternalOutput").ap()
        p.op("gpsimd", lambda e: e.collective_compute(
            "ReduceScatter", ALU.add, replica_groups=[list(range(NCORES))],
            ins=[y_part.opt()], outs=[y_own.opt()]),
            waits=[(s_sc, s_sc.n), (s_fz, s_fz.n)], inc=s_col)
        rs_col = s_col.n
        if debug:
            dbg7 = nc.dram_tensor("dbg_yp", [T, D], BF16, kind="ExternalOutput").ap()
            dbg8 = nc.dram_tensor("dbg_st2", [LT * 128, D], F32, kind="ExternalOutput").ap()
            p.op("gpsimd", lambda e: e.dma_start(out=dbg7, in_=y_part),
                 waits=[(s_sc, s_sc.n), (s_fz, s_fz.n)], inc=s_st, inc_n=16)
            for i in range(LT):
                p.op("sync", lambda e, i=i: e.dma_start(
                    out=dbg8[i * 128:(i + 1) * 128, :], in_=stream[i][:]),
                    waits=[(s_v, s_v.n)], inc=s_st, inc_n=16)
            p.wait("vector", (s_st, s_st.n))
        with ExitStack() as ces:
            yadd_v = {}
            yb = [ces.enter_context(nc.sbuf_tensor(f"yb{j}", [128, D], BF16))
                  for j in range(2)]
            for i in range(LT):
                p.op("sync", lambda e, i=i: e.dma_start(
                    out=yb[i % 2][:], in_=y_own[i * 128:(i + 1) * 128, :]),
                    waits=[(s_col, rs_col)] if i < 2 else [(s_v, yadd_v[i - 2])],
                    inc=s_ld, inc_n=16)
                yv = s_ld.n
                p.op("vector", lambda e, i=i: e.tensor_add(
                    out=stream[i][:], in0=stream[i][:], in1=yb[i % 2][:]),
                    waits=[(s_ld, yv), (s_v, s_v.n), (s_s, s_s.n)], inc=s_v)
                yadd_v[i] = s_v.n
                emit_layernorm(p, stream[i][:], lnscr[:], sqdump[:], col8[:],
                               s_v, s_s)
                p.op("sync", lambda e, i=i: e.dma_start(
                    out=out_own[i * 128:(i + 1) * 128, :], in_=stream[i][:]),
                    waits=[(s_v, s_v.n), (s_s, s_s.n)], inc=s_st, inc_n=16)

        p.wait("sync", (s_st, s_st.n))
        p.wait("gpsimd", (s_st, s_st.n))
        p.finish(blk)
    return nc


def kernel_fused(**inputs):
    per_core, shared, B_max = prep_host(inputs)
    nc = build_fused(B_max)
    in_maps = []
    for c in range(NCORES):
        m = {k: per_core[c][k] for k in ("x_own", "conv_x", "conv_w", "conv_gidx",
                                         "my_e", "w_ug_e", "w_dn_e")}
        for k in ("gate_w", "up_w", "down_w", "gate_s", "up_s", "down_s",
                  "scol_in", "wqkv", "bqkv", "wo", "bo", "router",
                  "w_sug", "w_sd"):
            m[k] = shared[k]
        m["masksel"] = prep_masksel(c)
        in_maps.append(m)
    res = run_spmd(nc, in_maps)
    return assemble_output([np.asarray(res.results[c]["out_own"])
                            for c in range(NCORES)]).astype(np.float32)


kernel = kernel_fused
